# version 17
# speedup vs baseline: 1.2651x; 1.0074x over previous
import numpy as np
import ml_dtypes
from contextlib import ExitStack

import concourse.bass as bass
import concourse.mybir as mybir
from concourse.bass_utils import run_bass_kernel_spmd

F32 = mybir.dt.float32
BF16 = mybir.dt.bfloat16
AF = mybir.ActivationFunctionType
ALU = mybir.AluOpType
AX = mybir.AxisListType

P = 128
D = 1024
T = 2048
KD = D // P
NTT = T // P
DS = 16
RK = 64
NE = 16
ED = 512
NET = ED // P
TCC = 256
NCH = T // TCC
TCW = 512
NCW = T // TCW

DEBUG = False


class Ctr:
    def __init__(self, stk, nc, name):
        self.h = stk.enter_context(nc.semaphore(name))
        self.v = 0

    def bump(self, k=1):
        self.v += k
        return self.v


class Sched:
    ENGS = ("pe", "act", "dve", "pool", "sp")

    def __init__(self):
        self.q = {e: [] for e in self.ENGS}

    def emit(self, eng, fn):
        self.q[eng].append(fn)

    def run_block(self, nc):
        qq = self.q
        with nc.Block() as blk:
            @blk.tensor
            def _(e):
                for f in qq["pe"]:
                    f(e)

            @blk.scalar
            def _(e):
                for f in qq["act"]:
                    f(e)

            @blk.vector
            def _(e):
                for f in qq["dve"]:
                    f(e)

            @blk.gpsimd
            def _(e):
                for f in qq["pool"]:
                    f(e)

            @blk.sync
            def _(e):
                for f in qq["sp"]:
                    f(e)
        self.q = {e: [] for e in self.ENGS}


def build_nc():
    nc = bass.Bass("TRN2", target_bir_lowering=False, detect_race_conditions=False)

    x_d = nc.dram_tensor("x", [T, D], F32, kind="ExternalInput").ap()
    alog_d = nc.dram_tensor("A_log", [D, DS], F32, kind="ExternalInput").ap()
    wx_d = nc.dram_tensor("W_x", [RK + DS, D], F32, kind="ExternalInput").ap()
    wdt_d = nc.dram_tensor("W_dt", [D, RK], F32, kind="ExternalInput").ap()
    bdt_d = nc.dram_tensor("b_dt", [D], F32, kind="ExternalInput").ap()
    wc_d = nc.dram_tensor("W_C", [DS, D], F32, kind="ExternalInput").ap()
    rw_d = nc.dram_tensor("router_w", [NE, D], F32, kind="ExternalInput").ap()
    rb_d = nc.dram_tensor("router_bias", [NE], F32, kind="ExternalInput").ap()
    up_d = nc.dram_tensor("up_proj", [NE, D, ED], F32, kind="ExternalInput").ap()
    gate_d = nc.dram_tensor("gate_proj", [NE, D, ED], F32, kind="ExternalInput").ap()
    down_d = nc.dram_tensor("down_proj", [NE, ED, D], F32, kind="ExternalInput").ap()
    shg_d = nc.dram_tensor("sh_gate", [D, D], F32, kind="ExternalInput").ap()
    shu_d = nc.dram_tensor("sh_up", [D, D], F32, kind="ExternalInput").ap()
    shd_d = nc.dram_tensor("sh_down", [D, D], F32, kind="ExternalInput").ap()
    idf_d = nc.dram_tensor("identf", [P, P], F32, kind="ExternalInput").ap()
    idb_d = nc.dram_tensor("identb", [P, P], BF16, kind="ExternalInput").ap()
    iota_d = nc.dram_tensor("iota16", [P, NE], F32, kind="ExternalInput").ap()
    iota100_d = nc.dram_tensor("iota100", [P, NE], F32, kind="ExternalInput").ap()
    e16_d = nc.dram_tensor("e16", [NE, NE * P], F32, kind="ExternalInput").ap()
    out_d = nc.dram_tensor("out", [T, D], F32, kind="ExternalOutput").ap()
    yscr_d = nc.dram_tensor("yscr", [P, KD * T], BF16, kind="Internal").ap()
    if DEBUG:
        dxn_d = nc.dram_tensor("d_xnT", [P, KD * T], F32, kind="ExternalOutput").ap()
        dcmb_d = nc.dram_tensor("d_cmb", [NE, T], F32, kind="ExternalOutput").ap()
        dacc_d = nc.dram_tensor("d_acc", [P, KD * T], F32, kind="ExternalOutput").ap()

    with ExitStack() as stk:
        sb = lambda n_, s_, d_: stk.enter_context(nc.sbuf_tensor(n_, s_, d_))
        xnT = sb("xnT", [P, KD * T], BF16)
        accb = sb("accb", [P, KD * T], BF16)
        rstdT = sb("rstdT", [P, NTT], F32)
        negmrT = sb("negmrT", [P, NTT], F32)
        Ab = sb("Ab", [P, KD * DS], F32)
        WxTb = sb("WxTb", [P, KD * (RK + DS)], BF16)
        WdtT = sb("WdtT", [RK, KD * P], BF16)
        WCT = sb("WCT", [P, KD * DS], BF16)
        bdtb = sb("bdtb", [P, KD], F32)
        idf = sb("idf", [P, P], F32)
        idb = sb("idb", [P, P], BF16)
        iota = sb("iota", [P, NE], F32)
        iota100 = sb("iota100s", [P, NE], F32)
        e16b = sb("e16b", [NE, NE * P], BF16)
        hcarry = sb("hcarry", [P, KD * DS], F32)
        cmb = sb("cmb", [NE, T], BF16)

        PE = Ctr(stk, nc, "PEs")
        AE = Ctr(stk, nc, "AEs")
        VE = Ctr(stk, nc, "VEs")
        PL = Ctr(stk, nc, "PLs")
        SPD = Ctr(stk, nc, "SPDd")
        GPD = Ctr(stk, nc, "GPDd")
        OUT = Ctr(stk, nc, "OUTd")
        YS = Ctr(stk, nc, "YSd")
        DBG = Ctr(stk, nc, "DBGd")

        S = Sched()

        with ExitStack() as psh:
            sbh = lambda n_, s_, d_: psh.enter_context(nc.sbuf_tensor(n_, s_, d_))
            shgT = sbh("shgT", [P, KD * D], BF16)
            shuT = sbh("shuT", [P, KD * D], BF16)
            shdT = sbh("shdT", [P, KD * D], BF16)

            with ExitStack() as pa:
                sba = lambda n_, s_, d_: pa.enter_context(nc.sbuf_tensor(n_, s_, d_))
                wxn = sba("wxn", [RK + DS, D], BF16)
                wdtn = sba("wdtn", [P, KD * RK], BF16)
                wcn = sba("wcn", [NE, D], BF16)
                rwn = sba("rwn", [NE, D], F32)
                alogn = sba("alogn", [P, KD * DS], F32)
                bdt8 = sba("bdt8", [KD, P], F32)
                shnat = sba("shnat", [P, KD * D], BF16)
                rbs = sba("rbs", [NE, 1], F32)
                xt = [sba(f"xt{i}", [P, D], F32) for i in range(3)]
                xsq = [sba(f"xsq{i}", [P, D], F32) for i in range(2)]
                xnbt = [sba(f"xnbt{i}", [P, D], BF16) for i in range(2)]
                xnft = [sba(f"xnft{i}", [P, D], F32) for i in range(2)]
                xfs8 = sba("xfs8", [P, D], F32)
                lgT = sba("lgT", [NE, T], F32)
                lgtm = sba("lgtm", [P, NTT * NE], F32)
                st1 = {nm: sba(nm, [P, NTT], F32)
                       for nm in ["s1c", "s2c", "mc", "qc", "vc", "m1c", "i1c",
                                  "m2c", "i2c", "w1c", "w2c", "mrc"]}
                scr16 = [sba(f"scr16_{i}", [P, NTT * NE], F32) for i in range(2)]
                ctm = sba("ctm", [P, NTT * NE], F32)

                pLog = pa.enter_context(nc.psum_tensor("pLog", [NE, T], F32))
                ptb8 = [pa.enter_context(nc.psum_tensor(f"ptb8{i}", [P, 1024], BF16))
                        for i in range(2)]
                ptf8 = pa.enter_context(nc.psum_tensor("ptf8", [P, 1024], F32))

                SPD.bump(6 * 16)
                consts_at = SPD.v

                def a1_sp(s):
                    for dst, src in [(idf, idf_d), (idb, idb_d), (iota, iota_d),
                                     (iota100, iota100_d)]:
                        s.dma_start(dst.ap(), src).then_inc(SPD.h, 16)
                    s.dma_start(rbs.ap(), rb_d.unsqueeze(1)).then_inc(SPD.h, 16)
                    s.dma_start(bdt8.ap(), bdt_d.rearrange("(c p) -> c p", p=P)
                                ).then_inc(SPD.h, 16)
                S.emit("sp", a1_sp)

                GPD.bump(7 * 16)
                w_loads = GPD.v

                def a1_gp(g):
                    g.dma_start(e16b.ap(), e16_d).then_inc(GPD.h, 16)
                    g.dma_start(wxn.ap(), wx_d).then_inc(GPD.h, 16)
                    g.dma_start(wdtn.ap().rearrange("p (c r) -> p c r", c=KD),
                                wdt_d.rearrange("(c p) r -> p c r", p=P)
                                ).then_inc(GPD.h, 16)
                    g.dma_start(wcn.ap(), wc_d).then_inc(GPD.h, 16)
                    g.dma_start(rwn.ap(), rw_d).then_inc(GPD.h, 16)
                    g.dma_start(alogn.ap().rearrange("p (c n) -> p c n", c=KD),
                                alog_d.rearrange("(c p) n -> p c n", p=P)
                                ).then_inc(GPD.h, 16)
                    g.dma_start(shnat.ap().rearrange("p (c d) -> p c d", c=KD),
                                shg_d.rearrange("(c p) d -> p c d", p=P)
                                ).then_inc(GPD.h, 16)
                S.emit("pool", a1_gp)

                rwT = sba("rwT", [P, KD * NE], F32)
                wt_groups = [
                    ("b0", [(wxn.ap()[:, j * P:(j + 1) * P], j * 80)
                            for j in range(KD)],
                     [(WxTb.ap(), P, 0, KD * 80)]),
                    ("b1", [(wdtn.ap()[:, j * RK:(j + 1) * RK], j * P)
                            for j in range(KD)],
                     [(WdtT.ap(), RK, 0, KD * P)]),
                    ("b0", [(wcn.ap()[:, j * P:(j + 1) * P], j * DS)
                            for j in range(KD)],
                     [(WCT.ap(), P, 0, KD * DS)]),
                    ("f", [(rwn.ap()[:, j * P:(j + 1) * P], j * NE)
                           for j in range(KD)] + [(bdt8.ap(), KD * NE)],
                     [(rwT.ap(), P, 0, KD * NE),
                      (bdtb.ap(), P, KD * NE, KD * NE + KD)]),
                ]
                wt_pe = {}
                wt_ve = {}
                for gi, (kind, srcs, dsts) in enumerate(wt_groups):
                    wt_pe[gi] = PE.bump(len(srcs))
                for gi, (kind, srcs, dsts) in enumerate(wt_groups):
                    wt_ve[gi] = VE.bump(len(dsts))

                def wt_ps(kind):
                    return {"b0": ptb8[0], "b1": ptb8[1], "f": ptf8}[kind]

                def a2_pe(t):
                    t.wait_ge(GPD.h, w_loads)
                    t.wait_ge(SPD.h, consts_at)
                    seen = {}
                    for gi, (kind, srcs, dsts) in enumerate(wt_groups):
                        if kind in seen:
                            t.wait_ge(VE.h, wt_ve[seen[kind]])
                        seen[kind] = gi
                        ps = wt_ps(kind)
                        for sap, coff in srcs:
                            npart = sap.shape[0]
                            ident = idb if kind != "f" else idf
                            t.transpose(ps.ap()[0:sap.shape[1], coff:coff + npart],
                                        sap, ident.ap()[0:npart, 0:npart]
                                        ).then_inc(PE.h, 1)
                S.emit("pe", a2_pe)

                def a2_dve(v):
                    for gi, (kind, srcs, dsts) in enumerate(wt_groups):
                        ps = wt_ps(kind)
                        v.wait_ge(PE.h, wt_pe[gi])
                        for dap, rows, lo, hi in dsts:
                            v.tensor_copy(dap, ps.ap()[0:rows, lo:hi]
                                          ).then_inc(VE.h, 1)
                S.emit("dve", a2_dve)

                aexp_at = AE.bump()

                def a2_act(a):
                    a.wait_ge(GPD.h, w_loads)
                    a.activation(Ab.ap(), alogn.ap(), AF.Exp).then_inc(AE.h, 1)
                S.emit("act", a2_act)

                aneg_at = VE.bump()

                def a2_dve2(v):
                    v.wait_ge(AE.h, aexp_at)
                    v.tensor_scalar(Ab.ap(), Ab.ap(), -1.0, None, ALU.mult
                                    ).then_inc(VE.h, 1)
                S.emit("dve", a2_dve2)

                A3 = {tau: {} for tau in range(NTT)}
                for tau in range(NTT):
                    A3[tau]["x"] = SPD.bump(16)
                for tau in range(NTT):
                    d = A3[tau]
                    d["sq"] = AE.bump()
                    d["ln"] = AE.bump()
                    d["rstd"] = AE.bump()
                    d["xnb"] = AE.bump()
                    d["xnf"] = AE.bump()
                for tau in range(NTT):
                    d = A3[tau]
                    for nm in ["s1", "s2", "m", "q", "v1", "v2", "stats",
                               "mr1", "mr", "bevac", "fevac"]:
                        d[nm] = VE.bump()
                for tau in range(NTT):
                    d = A3[tau]
                    d["btr"] = PE.bump(KD)
                    d["ftr"] = PE.bump(KD)
                    d["lmm"] = [PE.bump() for _ in range(KD)]

                def a3_sp(s):
                    for tau in range(NTT):
                        if tau >= 3:
                            s.wait_ge(AE.h, A3[tau - 3]["xnf"])
                            s.wait_ge(VE.h, A3[tau - 3]["stats"])
                        s.dma_start(xt[tau % 3].ap(), x_d[tau * P:(tau + 1) * P, :]
                                    ).then_inc(SPD.h, 16)
                S.emit("sp", a3_sp)

                def a3_act(a):
                    for tau in range(NTT):
                        xa = xt[tau % 3].ap()
                        a.wait_ge(SPD.h, A3[tau]["x"])
                        if tau >= 2:
                            a.wait_ge(VE.h, A3[tau - 2]["s2"])
                        a.activation(xsq[tau % 2].ap(), xa, AF.Square
                                     ).then_inc(AE.h, 1)
                        a.wait_ge(VE.h, A3[tau]["stats"])
                        vcol = st1["vc"].ap()[:, tau:tau + 1]
                        a.activation(st1["qc"].ap()[:, tau:tau + 1], vcol, AF.Ln
                                     ).then_inc(AE.h, 1)
                        a.wait_ge(AE.h, A3[tau]["ln"])
                        a.activation(rstdT.ap()[:, tau:tau + 1],
                                     st1["qc"].ap()[:, tau:tau + 1], AF.Exp,
                                     scale=-0.5).then_inc(AE.h, 1)
                        a.wait_ge(VE.h, A3[tau]["mr"])
                        if tau >= 2:
                            a.wait_ge(PE.h, A3[tau - 2]["btr"])
                        if tau >= 2:
                            a.wait_ge(PE.h, A3[tau - 2]["ftr"])
                        a.activation(xnbt[tau % 2].ap(), xa, AF.Identity,
                                     bias=negmrT.ap()[:, tau:tau + 1],
                                     scale=rstdT.ap()[:, tau:tau + 1]
                                     ).then_inc(AE.h, 1)
                        a.activation(xnft[tau % 2].ap(), xa, AF.Identity,
                                     bias=negmrT.ap()[:, tau:tau + 1],
                                     scale=rstdT.ap()[:, tau:tau + 1]
                                     ).then_inc(AE.h, 1)
                S.emit("act", a3_act)

                def a3_dve(v):
                    inv = 1.0 / D
                    for tau in range(NTT):
                        xa = xt[tau % 3].ap()
                        tc = lambda n: st1[n].ap()[:, tau:tau + 1]
                        d = A3[tau]
                        v.wait_ge(SPD.h, d["x"])
                        v.reduce_sum(tc("s1c"), xa, axis=AX.X).then_inc(VE.h, 1)
                        v.wait_ge(AE.h, d["sq"])
                        v.reduce_sum(tc("s2c"), xsq[tau % 2].ap(), axis=AX.X
                                     ).then_inc(VE.h, 1)
                        v.wait_ge(VE.h, d["s1"])
                        v.tensor_scalar(tc("mc"), tc("s1c"), inv, None, ALU.mult
                                        ).then_inc(VE.h, 1)
                        v.wait_ge(VE.h, d["s2"])
                        v.tensor_scalar(tc("qc"), tc("s2c"), inv, None, ALU.mult
                                        ).then_inc(VE.h, 1)
                        v.wait_ge(VE.h, d["m"])
                        v.tensor_tensor(tc("vc"), tc("mc"), tc("mc"), ALU.mult
                                        ).then_inc(VE.h, 1)
                        v.wait_ge(VE.h, max(d["q"], d["v1"]))
                        v.tensor_tensor(tc("vc"), tc("qc"), tc("vc"), ALU.subtract
                                        ).then_inc(VE.h, 1)
                        v.wait_ge(VE.h, d["v2"])
                        v.tensor_scalar(tc("vc"), tc("vc"), 1e-5, None, ALU.add
                                        ).then_inc(VE.h, 1)
                        v.wait_ge(AE.h, d["rstd"])
                        v.tensor_tensor(tc("mrc"), tc("mc"),
                                        rstdT.ap()[:, tau:tau + 1], ALU.mult
                                        ).then_inc(VE.h, 1)
                        v.wait_ge(VE.h, d["mr1"])
                        v.tensor_scalar(negmrT.ap()[:, tau:tau + 1], tc("mrc"),
                                        -1.0, None, ALU.mult).then_inc(VE.h, 1)
                        v.wait_ge(PE.h, d["btr"])
                        v.tensor_copy(
                            xnT.ap().rearrange("p (j t2) -> p j t2", t2=T)[
                                :, :, tau * P:(tau + 1) * P],
                            ptb8[tau % 2].ap().rearrange("p (j t2) -> p j t2", t2=P)
                        ).then_inc(VE.h, 1)
                        v.wait_ge(PE.h, d["ftr"])
                        if tau >= 1:
                            v.wait_ge(PE.h, A3[tau - 1]["lmm"][KD - 1])
                        v.tensor_copy(xfs8.ap(), ptf8.ap()).then_inc(VE.h, 1)
                S.emit("dve", a3_dve)

                def a3_pe(t):
                    for tau in range(NTT):
                        d = A3[tau]
                        t.wait_ge(AE.h, d["xnb"])
                        if tau >= 2:
                            t.wait_ge(VE.h, A3[tau - 2]["bevac"])
                        for j in range(KD):
                            t.transpose(ptb8[tau % 2].ap()[:, j * P:(j + 1) * P],
                                        xnbt[tau % 2].ap()[:, j * P:(j + 1) * P],
                                        idb.ap()).then_inc(PE.h, 1)
                        t.wait_ge(AE.h, d["xnf"])
                        if tau >= 1:
                            t.wait_ge(VE.h, A3[tau - 1]["fevac"])
                        for j in range(KD):
                            t.transpose(ptf8.ap()[:, j * P:(j + 1) * P],
                                        xnft[tau % 2].ap()[:, j * P:(j + 1) * P],
                                        idf.ap()).then_inc(PE.h, 1)
                        t.wait_ge(VE.h, d["fevac"])
                        for j in range(KD):
                            t.matmul(pLog.ap()[:, tau * P:(tau + 1) * P],
                                     rwT.ap()[:, j * NE:(j + 1) * NE],
                                     xfs8.ap()[:, j * P:(j + 1) * P],
                                     start=(j == 0), stop=(j == KD - 1)
                                     ).then_inc(PE.h, 1)
                S.emit("pe", a3_pe)

                lgb_at = AE.bump()

                def a4_act(a):
                    a.wait_ge(PE.h, A3[NTT - 1]["lmm"][KD - 1])
                    a.activation(lgT.ap(), pLog.ap(), AF.Identity, bias=rbs.ap()
                                 ).then_inc(AE.h, 1)
                S.emit("act", a4_act)

                lgtr = PE.bump(NTT)
                lgev = VE.bump()

                def a4_pe(t):
                    t.wait_ge(AE.h, lgb_at)
                    for tau in range(NTT):
                        t.transpose(ptf8.ap()[:, tau * NE:(tau + 1) * NE],
                                    lgT.ap()[:, tau * P:(tau + 1) * P],
                                    idf.ap()[0:NE, 0:NE]).then_inc(PE.h, 1)
                S.emit("pe", a4_pe)

                def a4_dve(v):
                    v.wait_ge(PE.h, lgtr)
                    v.tensor_copy(lgtm.ap(), ptf8.ap()[:, 0:NTT * NE]
                                  ).then_inc(VE.h, 1)
                S.emit("dve", a4_dve)

                lgs = lgtm.ap()

                def batch_step(prereqs, opgen):
                    at = VE.bump()

                    def f(v):
                        for sh, val in prereqs:
                            v.wait_ge(sh, val)
                        last = None
                        for tau in range(NTT):
                            last = opgen(v, tau)
                        last.then_inc(VE.h, 1)
                    S.emit("dve", f)
                    return at

                sl = lambda t_, tau: t_[:, tau * NE:(tau + 1) * NE]
                col = lambda n_, tau: st1[n_].ap()[:, tau:tau + 1]

                m1_at = batch_step([(VE.h, lgev)],
                                   lambda v, tau: v.reduce_max(
                                       col("m1c", tau), sl(lgs, tau), axis=AX.X))
                eq1_at = batch_step([(VE.h, m1_at)],
                                    lambda v, tau: v.tensor_single_scalar(
                                        sl(scr16[0].ap(), tau), sl(lgs, tau),
                                        col("m1c", tau), ALU.is_equal))
                cand1_at = batch_step([(VE.h, eq1_at)],
                                      lambda v, tau: v.scalar_tensor_tensor(
                                          sl(scr16[1].ap(), tau),
                                          sl(scr16[0].ap(), tau),
                                          -100.0, iota100.ap(), ALU.mult, ALU.add))
                i1_at = batch_step([(VE.h, cand1_at)],
                                   lambda v, tau: v.tensor_reduce(
                                       col("i1c", tau), sl(scr16[1].ap(), tau),
                                       AX.X, ALU.min))
                mask1_at = batch_step([(VE.h, i1_at)],
                                      lambda v, tau: v.tensor_single_scalar(
                                          sl(scr16[0].ap(), tau), iota.ap(),
                                          col("i1c", tau), ALU.is_equal))
                l2_at = batch_step([(VE.h, mask1_at)],
                                   lambda v, tau: v.scalar_tensor_tensor(
                                       sl(scr16[1].ap(), tau),
                                       sl(scr16[0].ap(), tau),
                                       -1.0e30, sl(lgs, tau), ALU.mult, ALU.add))
                m2_at = batch_step([(VE.h, l2_at)],
                                   lambda v, tau: v.reduce_max(
                                       col("m2c", tau), sl(scr16[1].ap(), tau),
                                       axis=AX.X))
                eq2_at = batch_step([(VE.h, m2_at)],
                                    lambda v, tau: v.tensor_single_scalar(
                                        sl(ctm.ap(), tau), sl(scr16[1].ap(), tau),
                                        col("m2c", tau), ALU.is_equal))
                cand2_at = batch_step([(VE.h, eq2_at)],
                                      lambda v, tau: v.scalar_tensor_tensor(
                                          sl(ctm.ap(), tau), sl(ctm.ap(), tau),
                                          -100.0, iota100.ap(), ALU.mult, ALU.add))
                i2_at = batch_step([(VE.h, cand2_at)],
                                   lambda v, tau: v.tensor_reduce(
                                       col("i2c", tau), sl(ctm.ap(), tau),
                                       AX.X, ALU.min))

                dm_at = VE.bump()

                def s_dm(v):
                    v.wait_ge(VE.h, i2_at)
                    v.tensor_sub(st1["w1c"].ap(), st1["m1c"].ap(),
                                 st1["m2c"].ap()).then_inc(VE.h, 1)
                S.emit("dve", s_dm)

                sig_at = AE.bump(2)

                def s_sig(a):
                    a.wait_ge(VE.h, dm_at)
                    a.activation(st1["w2c"].ap(), st1["w1c"].ap(), AF.Sigmoid,
                                 scale=-1.0).then_inc(AE.h, 1)
                    a.activation(st1["w1c"].ap(), st1["w1c"].ap(), AF.Sigmoid
                                 ).then_inc(AE.h, 1)
                S.emit("act", s_sig)

                mask2_at = batch_step([(AE.h, sig_at)],
                                      lambda v, tau: v.tensor_single_scalar(
                                          sl(scr16[1].ap(), tau), iota.ap(),
                                          col("i2c", tau), ALU.is_equal))
                cw_at = VE.bump()

                def s_cw(v):
                    v.wait_ge(VE.h, mask2_at)
                    for tau in range(NTT):
                        v.tensor_scalar(sl(ctm.ap(), tau), sl(scr16[0].ap(), tau),
                                        col("w1c", tau), None, ALU.mult)
                    last = None
                    for tau in range(NTT):
                        last = v.tensor_scalar(sl(scr16[1].ap(), tau),
                                               sl(scr16[1].ap(), tau),
                                               col("w2c", tau), None, ALU.mult)
                    last.then_inc(VE.h, 1)
                S.emit("dve", s_cw)

                comb_at = batch_step([(VE.h, cw_at)],
                                     lambda v, tau: v.tensor_add(
                                         sl(ctm.ap(), tau), sl(ctm.ap(), tau),
                                         sl(scr16[1].ap(), tau)))

                ctr_at = PE.bump(NTT)
                cev_at = VE.bump()

                def a4_pe2(t):
                    t.wait_ge(VE.h, comb_at)
                    for tau in range(NTT):
                        t.transpose(pLog.ap()[0:NE, tau * P:(tau + 1) * P],
                                    ctm.ap()[:, tau * NE:(tau + 1) * NE],
                                    idf.ap()).then_inc(PE.h, 1)
                S.emit("pe", a4_pe2)

                def a4_dve2(v):
                    v.wait_ge(PE.h, ctr_at)
                    v.tensor_copy(cmb.ap(), pLog.ap()[0:NE, :]).then_inc(VE.h, 1)
                S.emit("dve", a4_dve2)

                sh_pe = {}
                sh_ve = {}
                for w in range(3):
                    for dc in range(KD):
                        sh_pe[(w, dc)] = PE.bump(KD)
                for w in range(3):
                    for dc in range(KD):
                        sh_ve[(w, dc)] = VE.bump()
                shu_at = GPD.bump(16)
                shd_at = GPD.bump(16)

                def a5_gp(g):
                    g.wait_ge(PE.h, sh_pe[(0, KD - 1)])
                    g.dma_start(shnat.ap().rearrange("p (c d) -> p c d", c=KD),
                                shu_d.rearrange("(c p) d -> p c d", p=P)
                                ).then_inc(GPD.h, 16)
                    g.wait_ge(PE.h, sh_pe[(1, KD - 1)])
                    g.dma_start(shnat.ap().rearrange("p (c d) -> p c d", c=KD),
                                shd_d.rearrange("(c p) d -> p c d", p=P)
                                ).then_inc(GPD.h, 16)
                S.emit("pool", a5_gp)

                def a5_pe(t):
                    for w, dstT in enumerate([shgT, shuT, shdT]):
                        if w == 1:
                            t.wait_ge(GPD.h, shu_at)
                        if w == 2:
                            t.wait_ge(GPD.h, shd_at)
                        for dc in range(KD):
                            lin = w * KD + dc
                            if lin >= 2:
                                pw, pdc = divmod(lin - 2, KD)
                                t.wait_ge(VE.h, sh_ve[(pw, pdc)])
                            for hc in range(KD):
                                t.transpose(
                                    ptb8[dc % 2].ap()[:, hc * P:(hc + 1) * P],
                                    shnat.ap()[:, hc * D + dc * P:
                                               hc * D + (dc + 1) * P],
                                    idb.ap()).then_inc(PE.h, 1)
                S.emit("pe", a5_pe)

                def a5_dve(v):
                    for w, dstT in enumerate([shgT, shuT, shdT]):
                        for dc in range(KD):
                            v.wait_ge(PE.h, sh_pe[(w, dc)])
                            v.tensor_copy(dstT.ap()[:, dc * D:(dc + 1) * D],
                                          ptb8[dc % 2].ap()).then_inc(VE.h, 1)
                S.emit("dve", a5_dve)

                if DEBUG:
                    DBG.bump(32)
                    vnow = VE.v

                    def dbg_a(g):
                        g.wait_ge(VE.h, vnow)
                        g.dma_start(dcmb_d, cmb.ap()).then_inc(DBG.h, 16)
                        g.dma_start(dxn_d, xnT.ap()).then_inc(DBG.h, 16)
                        g.wait_ge(DBG.h, 32)
                    S.emit("pool", dbg_a)

                S.run_block(nc)

            shh = sbh("shh", [P, KD * T], BF16)
            sigbg = [sbh(f"sigbg{i}", [P, 1024], BF16) for i in range(2)]
            gPA = psh.enter_context(nc.psum_tensor("gPA", [P, 1024], F32))
            gPB = psh.enter_context(nc.psum_tensor("gPB", [P, 1024], F32))
            gPC = [psh.enter_context(nc.psum_tensor(f"gPC{i}", [P, 512], F32))
                   for i in range(2)]

            glu = {}
            for th in range(2):
                for ht in range(KD):
                    glu[("g", th, ht)] = PE.bump(16)
                    glu[("u", th, ht)] = PE.bump(16)
                for dt in range(KD):
                    for q in range(2):
                        glu[("d", th, dt, q)] = PE.bump(KD)
            for th in range(2):
                for ht in range(KD):
                    glu[("sig", th, ht)] = AE.bump()
            for th in range(2):
                for ht in range(KD):
                    glu[("shh", th, ht)] = VE.bump()
                for dt in range(KD):
                    for q in range(2):
                        glu[("acc", th, dt, q)] = VE.bump()

            def prev_ht(th, ht, k):
                lin = th * KD + ht - k
                return (lin // KD, lin % KD) if lin >= 0 else None

            def b1_pe(t):
                for th in range(2):
                    for ht in range(KD):
                        pv = prev_ht(th, ht, 1)
                        if pv is not None:
                            t.wait_ge(AE.h, glu[("sig",) + pv])
                        for dc in range(KD):
                            for q in range(2):
                                t.matmul(gPA.ap()[:, q * 512:(q + 1) * 512],
                                         shgT.ap()[:, dc * D + ht * P:
                                                   dc * D + (ht + 1) * P],
                                         xnT.ap()[:, dc * T + th * 1024 + q * 512:
                                                  dc * T + th * 1024 + (q + 1) * 512],
                                         start=(dc == 0), stop=(dc == KD - 1)
                                         ).then_inc(PE.h, 1)
                        if pv is not None:
                            t.wait_ge(VE.h, glu[("shh",) + pv])
                        for dc in range(KD):
                            for q in range(2):
                                t.matmul(gPB.ap()[:, q * 512:(q + 1) * 512],
                                         shuT.ap()[:, dc * D + ht * P:
                                                   dc * D + (ht + 1) * P],
                                         xnT.ap()[:, dc * T + th * 1024 + q * 512:
                                                  dc * T + th * 1024 + (q + 1) * 512],
                                         start=(dc == 0), stop=(dc == KD - 1)
                                         ).then_inc(PE.h, 1)
                    t.wait_ge(VE.h, glu[("shh", th, KD - 1)])
                    for dt in range(KD):
                        for q in range(2):
                            lin = (th * KD + dt) * 2 + q
                            if lin >= 2:
                                plin = lin - 2
                                pth = plin // (2 * KD)
                                pr = plin % (2 * KD)
                                t.wait_ge(VE.h, glu[("acc", pth, pr // 2, pr % 2)])
                            for hc in range(KD):
                                t.matmul(gPC[lin % 2].ap(),
                                         shdT.ap()[:, hc * D + dt * P:
                                                   hc * D + (dt + 1) * P],
                                         shh.ap()[:, hc * T + th * 1024 + q * 512:
                                                  hc * T + th * 1024 + (q + 1) * 512],
                                         start=(hc == 0), stop=(hc == KD - 1)
                                         ).then_inc(PE.h, 1)
            S.emit("pe", b1_pe)

            def b1_act(a):
                for th in range(2):
                    for ht in range(KD):
                        a.wait_ge(PE.h, glu[("g", th, ht)])
                        pv = prev_ht(th, ht, 2)
                        if pv is not None:
                            a.wait_ge(VE.h, glu[("shh",) + pv])
                        a.activation(sigbg[ht % 2].ap(), gPA.ap(), AF.Silu
                                     ).then_inc(AE.h, 1)
            S.emit("act", b1_act)

            def b1_dve(v):
                for th in range(2):
                    for ht in range(KD):
                        v.wait_ge(PE.h, glu[("u", th, ht)])
                        v.wait_ge(AE.h, glu[("sig", th, ht)])
                        v.tensor_tensor(shh.ap()[:, ht * T + th * 1024:
                                                 ht * T + (th + 1) * 1024],
                                        gPB.ap(), sigbg[ht % 2].ap(), ALU.mult
                                        ).then_inc(VE.h, 1)
                    for dt in range(KD):
                        for q in range(2):
                            lin = (th * KD + dt) * 2 + q
                            v.wait_ge(PE.h, glu[("d", th, dt, q)])
                            v.tensor_copy(
                                accb.ap()[:, dt * T + th * 1024 + q * 512:
                                          dt * T + th * 1024 + (q + 1) * 512],
                                gPC[lin % 2].ap()).then_inc(VE.h, 1)
            S.emit("dve", b1_dve)

            S.run_block(nc)

        with ExitStack() as pm3:
            sbm = lambda n_, s_, d_: pm3.enter_context(nc.sbuf_tensor(n_, s_, d_))
            delt = sbm("delt", [P, KD * TCC], F32)
            Bball = sbm("Bball", [P, DS * TCC], BF16)
            Cball = sbm("Cball", [P, DS * TCC], BF16)
            ball = sbm("ball", [P, DS * TCC], BF16)
            yb8 = sbm("yb8", [P, KD * TCC], BF16)
            Btf = sbm("Btf", [NE, TCC], BF16)
            Ctf = sbm("Ctf", [NE, TCC], BF16)
            dlowb = sbm("dlowb", [RK, TCC], BF16)
            ub1 = sbm("ub1", [P, TCC], BF16)
            spscr = [sbm(f"spscr{i}", [P, TCC], F32) for i in range(2)]
            abuf = [sbm(f"abuf{i}", [P, TCC], F32) for i in range(6)]
            upw = [sbm(f"upw{i}", [P, KD * ED], BF16) for i in range(2)]
            gatew = [sbm("gatew0", [P, KD * ED], BF16)]
            downw = [sbm(f"downw{i}", [P, NET * D], BF16) for i in range(2)]
            hidp = [sbm(f"hidp{i}", [P, NET * T], BF16) for i in range(2)]
            cbf = sbm("cbf", [P, T], BF16)
            sigb2 = [sbm(f"sigb2{i}", [P, 1024], BF16) for i in range(2)]
            upcb = [sbm(f"upcb{i}", [P, 1024], BF16) for i in range(2)]

            U = [pm3.enter_context(nc.psum_tensor(f"U{i}", [P, 512], F32))
                 for i in range(4)]
            PB = pm3.enter_context(nc.psum_tensor("PB", [P, 1024], F32))
            Pm = [pm3.enter_context(nc.psum_tensor(f"Pm{i}", [P, TCC], F32))
                  for i in range(2)]

            hz_at = VE.bump()

            def m_init(v):
                v.memset(hcarry.ap(), 0.0).then_inc(VE.h, 1)
                v.wait_ge(VE.h, hz_at)
            S.emit("dve", m_init)

            EX = {}
            C = {}
            for k in range(NE):
                EX[("wug", k)] = GPD.bump(32)
                EX[("wdn", k)] = GPD.bump(16)

            useq = []

            def pe_expert_marks(k, dpre_c=None, ki=0):
                for h in range(4):
                    EX[("cb", k, h)] = PE.bump()
                    useq.append(("AE", ("cbe", k, h)))
                for s in range(KD):
                    for q in range(2):
                        EX[("up", k, s, q)] = PE.bump(KD)
                        useq.append(("VE", ("upch", k, s, q)))
                    EX[("gate", k, s)] = PE.bump(16)
                    if dpre_c is not None and s % 2 == 1:
                        j = ki * 4 + (s - 1) // 2
                        if j <= 5:
                            C[("dpre", dpre_c, j + 2)] = PE.bump()

            for slot in range(KD):
                c = slot
                dc = c - 1 if c >= 1 else None
                pe_expert_marks(2 * slot, dpre_c=dc, ki=0)
                pe_expert_marks(2 * slot + 1, dpre_c=dc, ki=1)
                kk = 2 * slot + 1

                def pd_mark(lin):
                    EX[("pd", kk, lin)] = PE.bump(KD)
                    useq.append(("VE", ("pa", kk, lin)))

                C[("bcB", c)] = PE.bump(KD)
                C[("bcC", c)] = PE.bump(KD)
                C[("dlow", c)] = PE.bump(KD)
                for lin in range(8):
                    pd_mark(lin)
                for i in range(KD):
                    C[("bbm", c, 2 * i)] = PE.bump()
                    C[("bbm", c, 2 * i + 1)] = PE.bump()
                    pd_mark(8 + i)
                for i in range(KD):
                    C[("cbm", c, 2 * i)] = PE.bump()
                    C[("cbm", c, 2 * i + 1)] = PE.bump()
                    pd_mark(16 + i)
                for dt in range(2):
                    C[("dpre", c, dt)] = PE.bump()
                for lin in range(24, 32):
                    pd_mark(lin)
            for dt in range(2, KD):
                C[("dpre", NCH - 1, dt)] = PE.bump()

            def ae_sp_marks(c, dt):
                C[("sp1", c, dt)] = AE.bump()
                C[("sp2", c, dt)] = AE.bump()
                C[("sp3", c, dt)] = AE.bump()

            def ae_a_marks(c, dt):
                for n in range(DS):
                    C[("a", c, dt, n)] = AE.bump()

            def ae_ga_marks(c, dt):
                for n in range(6):
                    C[("a", c, dt, n)] = AE.bump()
                if dt < KD - 1:
                    C[("sp1", c, dt + 1)] = AE.bump()
                C[("a", c, dt, 6)] = AE.bump()
                if dt < KD - 1:
                    C[("sp2", c, dt + 1)] = AE.bump()
                C[("a", c, dt, 7)] = AE.bump()
                if dt < KD - 1:
                    C[("sp3", c, dt + 1)] = AE.bump()
                for n in range(8, DS):
                    C[("a", c, dt, n)] = AE.bump()

            for slot in range(KD):
                c = slot
                for ki, k in enumerate((2 * slot, 2 * slot + 1)):
                    for h in range(4):
                        EX[("cbe", k, h)] = AE.bump()
                    for s in range(KD):
                        EX[("sig", k, s)] = AE.bump()
                        if c >= 1 and s % 2 == 0:
                            ae_ga_marks(c - 1, ki * 4 + s // 2)
                C[("Bev", c)] = AE.bump()
                C[("Cev", c)] = AE.bump()
                C[("dlev", c)] = AE.bump()
                for n in range(DS):
                    C[("bbe", c, n)] = AE.bump()
                for n in range(DS):
                    C[("cbe", c, n)] = AE.bump()
                ae_sp_marks(c, 0)
            for dt in range(KD):
                ae_ga_marks(NCH - 1, dt)
            gfin = AE.bump(KD)

            def ve_group_marks(c, dt, e6):
                if e6 is not None:
                    EX[e6[0]] = VE.bump()
                    EX[e6[1]] = VE.bump()
                C[("del", c, dt)] = VE.bump()
                C[("u", c, dt)] = VE.bump()
                C[("ball", c, dt)] = VE.bump()
                if e6 is not None:
                    EX[e6[2]] = VE.bump()
                if dt < KD - 1:
                    C[("r", c, dt + 1)] = VE.bump()
                for n in range(0, 3):
                    C[("scan", c, dt, n)] = VE.bump()
                if e6 is not None:
                    EX[e6[3]] = VE.bump()
                    EX[e6[4]] = VE.bump()
                for n in range(3, DS):
                    C[("scan", c, dt, n)] = VE.bump()
                if e6 is not None:
                    EX[e6[5]] = VE.bump()
                C[("carry", c, dt)] = VE.bump()
                C[("M", c, dt)] = VE.bump()
                C[("t1", c, dt)] = VE.bump()
                C[("t2", c, dt)] = VE.bump()
                C[("t3", c, dt)] = VE.bump()
                C[("red", c, dt)] = VE.bump()

            def ve_ev_marks(slot):
                marks = []
                for k in (2 * slot, 2 * slot + 1):
                    for s in range(KD):
                        marks.append(("upch", k, s, 0))
                        marks.append(("upch", k, s, 1))
                        marks.append(("hid", k, s))
                return marks

            def ve_pa_marks(slot):
                kk = 2 * slot + 1
                return [("pa", kk, lin) for lin in range(32)]

            for slot in range(KD):
                c = slot
                ev = ve_ev_marks(slot)
                if c == 0:
                    for m in ev:
                        EX[m] = VE.bump()
                else:
                    for dt in range(KD):
                        ve_group_marks(c - 1, dt, ev[6 * dt:6 * dt + 6])
                for m in ve_pa_marks(slot):
                    EX[m] = VE.bump()
                C[("r", c, 0)] = VE.bump()
            for dt in range(KD):
                ve_group_marks(NCH - 1, dt, None)
            for c in range(NCH):
                C[("ys", c)] = YS.bump(16)

            def pdlast(k):
                o = k - 1 if k % 2 == 0 else k - 2
                return EX[("pd", o, 31)] if o >= 1 else None

            def m_pool(g):
                for k in range(NE):
                    b = k % 2
                    if k >= 2:
                        g.wait_ge(PE.h, EX[("gate", k - 2, KD - 1)])
                    g.dma_start(upw[b].ap().rearrange("p (c e) -> p c e", c=KD),
                                up_d[k].rearrange("(c p) e -> p c e", p=P)
                                ).then_inc(GPD.h, 16)
                    if k >= 1:
                        g.wait_ge(PE.h, EX[("gate", k - 1, KD - 1)])
                    g.dma_start(gatew[0].ap().rearrange("p (c e) -> p c e", c=KD),
                                gate_d[k].rearrange("(c p) e -> p c e", p=P)
                                ).then_inc(GPD.h, 16)
                    if k >= 2:
                        g.wait_ge(PE.h, pdlast(k))
                    g.dma_start(downw[b].ap().rearrange("p (c d) -> p c d", c=NET),
                                down_d[k].rearrange("(c p) d -> p c d", p=P)
                                ).then_inc(GPD.h, 16)
            S.emit("pool", m_pool)

            uidx = [0]

            def u_wait(t):
                i = uidx[0]
                if i >= 4:
                    eng, key = useq[i - 4]
                    if eng == "AE":
                        t.wait_ge(AE.h, EX[key])
                    else:
                        t.wait_ge(VE.h, EX[key])
                uidx[0] += 1
                return U[i % 4]

            def pe_expert(t, k, dpre_c=None, ki=0):
                b = k % 2
                t.wait_ge(GPD.h, EX[("wug", k)])
                for h in range(4):
                    th, q = divmod(h, 2)
                    ut = u_wait(t)
                    t.matmul(ut.ap(),
                             e16b.ap()[:, k * P:(k + 1) * P],
                             cmb.ap()[:, th * 1024 + q * 512:
                                      th * 1024 + (q + 1) * 512],
                             start=True, stop=True).then_inc(PE.h, 1)
                for s in range(KD):
                    et, th = s // 2, s % 2
                    for q in range(2):
                        ut = u_wait(t)
                        for kc in range(KD):
                            t.matmul(
                                ut.ap(),
                                upw[b].ap()[:, kc * ED + et * P:
                                            kc * ED + (et + 1) * P],
                                xnT.ap()[:, kc * T + th * 1024 + q * 512:
                                         kc * T + th * 1024 + (q + 1) * 512],
                                start=(kc == 0), stop=(kc == KD - 1)
                            ).then_inc(PE.h, 1)
                    if s >= 1:
                        t.wait_ge(AE.h, EX[("sig", k, s - 1)])
                    elif k >= 1:
                        t.wait_ge(AE.h, EX[("sig", k - 1, KD - 1)])
                    for kc in range(KD):
                        for q in range(2):
                            t.matmul(
                                PB.ap()[:, q * 512:(q + 1) * 512],
                                gatew[0].ap()[:, kc * ED + et * P:
                                              kc * ED + (et + 1) * P],
                                xnT.ap()[:, kc * T + th * 1024 + q * 512:
                                         kc * T + th * 1024 + (q + 1) * 512],
                                start=(kc == 0), stop=(kc == KD - 1)
                            ).then_inc(PE.h, 1)
                    if dpre_c is not None and s % 2 == 1:
                        j = ki * 4 + (s - 1) // 2
                        if j <= 5:
                            pe_dpre(t, dpre_c, j + 2)

            def pe_dpre(t, c, dt):
                if dt < 2:
                    t.wait_ge(AE.h, C[("cbe", c, DS - 2 + dt)])
                else:
                    t.wait_ge(AE.h, C[("sp1", c, dt - 2)])
                    t.wait_ge(VE.h, C[("r", c, dt - 2)])
                t.matmul(Pm[dt % 2].ap(),
                         WdtT.ap()[:, dt * P:(dt + 1) * P], dlowb.ap(),
                         start=True, stop=True).then_inc(PE.h, 1)

            def m_pe(t):
                for slot in range(KD):
                    c = slot
                    dc = c - 1 if c >= 1 else None
                    pe_expert(t, 2 * slot, dpre_c=dc, ki=0)
                    pe_expert(t, 2 * slot + 1, dpre_c=dc, ki=1)
                    kk = 2 * slot + 1

                    def pe_down_lin(lin):
                        dt, r = divmod(lin, 4)
                        th, q = divmod(r, 2)
                        if lin == 0:
                            t.wait_ge(VE.h, EX[("hid", kk, KD - 1)])
                        ut = u_wait(t)
                        first = True
                        for half, ke in ((0, kk - 1), (1, kk)):
                            for ec in range(NET):
                                t.matmul(
                                    ut.ap(),
                                    downw[ke % 2].ap()[:, ec * D + dt * P:
                                                       ec * D + (dt + 1) * P],
                                    hidp[ke % 2].ap()[
                                        :, ec * T + th * 1024 + q * 512:
                                        ec * T + th * 1024 + (q + 1) * 512],
                                    start=first,
                                    stop=(half == 1 and ec == NET - 1)
                                ).then_inc(PE.h, 1)
                                first = False

                    if c >= 1:
                        t.wait_ge(AE.h, C[("sp1", c - 1, KD - 1)])
                        t.wait_ge(VE.h, C[("r", c - 1, KD - 1)])
                    for kc in range(KD):
                        t.matmul(Pm[0].ap()[0:NE, :],
                                 WxTb.ap()[:, kc * 80 + RK:kc * 80 + 80],
                                 xnT.ap()[:, kc * T + c * TCC:kc * T + (c + 1) * TCC],
                                 start=(kc == 0), stop=(kc == KD - 1)
                                 ).then_inc(PE.h, 1)
                    for kc in range(KD):
                        t.matmul(Pm[1].ap()[0:NE, :],
                                 WCT.ap()[:, kc * DS:(kc + 1) * DS],
                                 xnT.ap()[:, kc * T + c * TCC:kc * T + (c + 1) * TCC],
                                 start=(kc == 0), stop=(kc == KD - 1)
                                 ).then_inc(PE.h, 1)
                    t.wait_ge(AE.h, C[("Bev", c)])
                    for kc in range(KD):
                        t.matmul(Pm[0].ap()[0:RK, :],
                                 WxTb.ap()[:, kc * 80:kc * 80 + RK],
                                 xnT.ap()[:, kc * T + c * TCC:kc * T + (c + 1) * TCC],
                                 start=(kc == 0), stop=(kc == KD - 1)
                                 ).then_inc(PE.h, 1)
                    t.wait_ge(GPD.h, EX[("wdn", kk)])
                    for lin in range(8):
                        pe_down_lin(lin)
                    for i in range(KD):
                        for n in (2 * i, 2 * i + 1):
                            if n == 0:
                                t.wait_ge(AE.h, C[("dlev", c)])
                            elif n == 1:
                                t.wait_ge(AE.h, C[("Cev", c)])
                            else:
                                t.wait_ge(AE.h, C[("bbe", c, n - 2)])
                            t.matmul(Pm[n % 2].ap(),
                                     e16b.ap()[:, n * P:(n + 1) * P], Btf.ap(),
                                     start=True, stop=True).then_inc(PE.h, 1)
                        pe_down_lin(8 + i)
                    for i in range(KD):
                        for n in (2 * i, 2 * i + 1):
                            if n < 2:
                                t.wait_ge(AE.h, C[("bbe", c, DS - 2 + n)])
                            else:
                                t.wait_ge(AE.h, C[("cbe", c, n - 2)])
                            t.matmul(Pm[n % 2].ap(),
                                     e16b.ap()[:, n * P:(n + 1) * P], Ctf.ap(),
                                     start=True, stop=True).then_inc(PE.h, 1)
                        pe_down_lin(16 + i)
                    for dt in range(2):
                        pe_dpre(t, c, dt)
                    for lin in range(24, 32):
                        pe_down_lin(lin)
                for dt in range(2, KD):
                    pe_dpre(t, NCH - 1, dt)
            S.emit("pe", m_pe)

            def ae_sp1(a, c, dt):
                a.wait_ge(PE.h, C[("dpre", c, dt)])
                a.wait_ge(VE.h, C[("r", c, dt)])
                if dt >= 1:
                    a.wait_ge(VE.h, C[("del", c, dt - 1)])
                elif c >= 1:
                    a.wait_ge(VE.h, C[("del", c - 1, KD - 1)])
                a.activation(spscr[0].ap(), Pm[dt % 2].ap(),
                             AF.Abs, bias=bdtb.ap()[:, dt:dt + 1]).then_inc(AE.h, 1)

            def ae_sp2(a, c, dt):
                a.wait_ge(AE.h, C[("sp1", c, dt)])
                a.activation(spscr[1].ap(), spscr[0].ap(), AF.Exp,
                             scale=-1.0).then_inc(AE.h, 1)

            def ae_sp3(a, c, dt):
                a.wait_ge(AE.h, C[("sp2", c, dt)])
                a.activation(spscr[0].ap(), spscr[1].ap(), AF.Ln,
                             bias=1.0).then_inc(AE.h, 1)

            def ae_a1(a, c, dt, n):
                lin = dt * DS + n
                if n == 0:
                    a.wait_ge(VE.h, C[("del", c, dt)])
                if lin >= 6:
                    pdt, pn = divmod(lin - 6, DS)
                    a.wait_ge(VE.h, C[("scan", c, pdt, pn)])
                elif c >= 1:
                    pdt, pn = divmod(KD * DS - 6 + lin, DS)
                    a.wait_ge(VE.h, C[("scan", c - 1, pdt, pn)])
                a.activation(abuf[lin % 6].ap(),
                             delt.ap()[:, dt * TCC:(dt + 1) * TCC],
                             AF.Exp,
                             scale=Ab.ap()[:, dt * DS + n:dt * DS + n + 1]
                             ).then_inc(AE.h, 1)

            def ae_ga(a, c, dt):
                for n in range(6):
                    ae_a1(a, c, dt, n)
                if dt < KD - 1:
                    ae_sp1(a, c, dt + 1)
                ae_a1(a, c, dt, 6)
                if dt < KD - 1:
                    ae_sp2(a, c, dt + 1)
                ae_a1(a, c, dt, 7)
                if dt < KD - 1:
                    ae_sp3(a, c, dt + 1)
                for n in range(8, DS):
                    ae_a1(a, c, dt, n)

            def m_act(a):
                for slot in range(KD):
                    c = slot
                    for ki, k in enumerate((2 * slot, 2 * slot + 1)):
                        for h in range(4):
                            th, q = divmod(h, 2)
                            a.wait_ge(PE.h, EX[("cb", k, h)])
                            a.copy(cbf.ap()[:, th * 1024 + q * 512:
                                            th * 1024 + (q + 1) * 512],
                                   U[h % 4].ap()).then_inc(AE.h, 1)
                        for s in range(KD):
                            a.wait_ge(PE.h, EX[("gate", k, s)])
                            lin = k * KD + s
                            if lin >= 2:
                                pk, ps_ = divmod(lin - 2, KD)
                                a.wait_ge(VE.h, EX[("hid", pk, ps_)])
                            a.activation(sigb2[s % 2].ap(), PB.ap(), AF.Silu
                                         ).then_inc(AE.h, 1)
                            if c >= 1 and s % 2 == 0:
                                ae_ga(a, c - 1, ki * 4 + s // 2)
                    a.wait_ge(PE.h, C[("bcB", c)])
                    a.copy(Btf.ap(), Pm[0].ap()[0:NE, :]).then_inc(AE.h, 1)
                    a.wait_ge(PE.h, C[("bcC", c)])
                    a.copy(Ctf.ap(), Pm[1].ap()[0:NE, :]).then_inc(AE.h, 1)
                    a.wait_ge(PE.h, C[("dlow", c)])
                    a.copy(dlowb.ap(), Pm[0].ap()[0:RK, :]).then_inc(AE.h, 1)
                    for n in range(DS):
                        if n == 0 and c >= 1:
                            a.wait_ge(VE.h, C[("ball", c - 1, KD - 1)])
                        a.wait_ge(PE.h, C[("bbm", c, n)])
                        a.copy(Bball.ap()[:, n * TCC:(n + 1) * TCC],
                               Pm[n % 2].ap()).then_inc(AE.h, 1)
                    for n in range(DS):
                        if n == 0 and c >= 1:
                            a.wait_ge(VE.h, C[("M", c - 1, KD - 1)])
                        a.wait_ge(PE.h, C[("cbm", c, n)])
                        a.copy(Cball.ap()[:, n * TCC:(n + 1) * TCC],
                               Pm[n % 2].ap()).then_inc(AE.h, 1)
                    ae_sp1(a, c, 0)
                    ae_sp2(a, c, 0)
                    ae_sp3(a, c, 0)
                for dt in range(KD):
                    ae_ga(a, NCH - 1, dt)
                a.wait_ge(VE.h, EX[("pa", NE - 1, 31)])
                for dt in range(KD):
                    a.activation(accb.ap()[:, dt * T:(dt + 1) * T],
                                 accb.ap()[:, dt * T:(dt + 1) * T], AF.Silu
                                 ).then_inc(AE.h, 1)
            S.emit("act", m_act)

            def ve_ev_item(v, m):
                kind = m[0]
                if kind == "upch":
                    _, k, s, q = m
                    th = s % 2
                    v.wait_ge(PE.h, EX[("up", k, s, q)])
                    if q == 0:
                        lin = k * KD + s
                        if lin >= 2:
                            pk, ps_ = divmod(lin - 2, KD)
                            v.wait_ge(VE.h, EX[("hid", pk, ps_)])
                    v.tensor_tensor(upcb[s % 2].ap()[:, q * 512:(q + 1) * 512],
                                    U[(2 * s + q) % 4].ap(),
                                    cbf.ap()[:, th * 1024 + q * 512:
                                             th * 1024 + (q + 1) * 512],
                                    ALU.mult).then_inc(VE.h, 1)
                elif kind == "hid":
                    _, k, s = m
                    et, th = s // 2, s % 2
                    v.wait_ge(AE.h, EX[("sig", k, s)])
                    if k % 2 == 0 and k >= 2 and s == 0:
                        v.wait_ge(PE.h, pdlast(k))
                    v.tensor_tensor(
                        hidp[k % 2].ap()[:, et * T + th * 1024:
                                         et * T + (th + 1) * 1024],
                        sigb2[s % 2].ap(), upcb[s % 2].ap(), ALU.mult
                    ).then_inc(VE.h, 1)
                else:
                    _, kk, lin = m
                    dt, r = divmod(lin, 4)
                    th, q = divmod(r, 2)
                    v.wait_ge(PE.h, EX[("pd", kk, lin)])
                    asl = accb.ap()[:, dt * T + th * 1024 + q * 512:
                                    dt * T + th * 1024 + (q + 1) * 512]
                    v.tensor_tensor(asl, U[lin % 4].ap(), asl, ALU.add
                                    ).then_inc(VE.h, 1)

            def ve_m_pre(v, c, dt):
                dsl = delt.ap()[:, dt * TCC:(dt + 1) * TCC]
                v.tensor_tensor(ub1.ap(), dsl,
                                xnT.ap()[:, dt * T + c * TCC:
                                         dt * T + (c + 1) * TCC],
                                ALU.mult).then_inc(VE.h, 1)
                v.wait_ge(AE.h, C[("bbe", c, DS - 1)])
                v.wait_ge(VE.h, C[("u", c, dt)])
                if dt >= 1:
                    v.wait_ge(VE.h, C[("red", c, dt - 1)])
                elif c >= 1:
                    v.wait_ge(VE.h, C[("red", c - 1, KD - 1)])
                v.tensor_tensor(
                    ball.ap().rearrange("p (n t) -> p n t", n=DS),
                    Bball.ap().rearrange("p (n t) -> p n t", n=DS),
                    ub1.ap().unsqueeze(1).broadcast_to([P, DS, TCC]),
                    ALU.mult).then_inc(VE.h, 1)

            def ve_m_scan(v, c, dt, n):
                v.wait_ge(VE.h, C[("ball", c, dt)])
                v.wait_ge(AE.h, C[("a", c, dt, n)])
                v.tensor_tensor_scan(
                    ball.ap()[:, n * TCC:(n + 1) * TCC],
                    abuf[(dt * DS + n) % 6].ap(),
                    ball.ap()[:, n * TCC:(n + 1) * TCC],
                    hcarry.ap()[:, dt * DS + n:dt * DS + n + 1],
                    ALU.mult, ALU.add).then_inc(VE.h, 1)

            def ve_m_post(v, c, dt):
                v.wait_ge(VE.h, C[("scan", c, dt, DS - 1)])
                v.tensor_copy(
                    hcarry.ap()[:, dt * DS:(dt + 1) * DS].unsqueeze(2),
                    ball.ap().rearrange("p (n t) -> p n t", n=DS)[
                        :, :, TCC - 1:TCC]).then_inc(VE.h, 1)
                v.wait_ge(AE.h, C[("cbe", c, DS - 1)])
                v.wait_ge(VE.h, C[("carry", c, dt)])
                v.tensor_tensor(ball.ap(), ball.ap(), Cball.ap(), ALU.mult
                                ).then_inc(VE.h, 1)
                hv = lambda a_, b_: ball.ap().rearrange(
                    "p (n t) -> p n t", n=DS)[:, a_:b_, :]
                v.wait_ge(VE.h, C[("M", c, dt)])
                v.tensor_tensor(hv(0, 8), hv(0, 8), hv(8, 16), ALU.add
                                ).then_inc(VE.h, 1)
                v.wait_ge(VE.h, C[("t1", c, dt)])
                v.tensor_tensor(hv(0, 4), hv(0, 4), hv(4, 8), ALU.add
                                ).then_inc(VE.h, 1)
                v.wait_ge(VE.h, C[("t2", c, dt)])
                v.tensor_tensor(hv(0, 2), hv(0, 2), hv(2, 4), ALU.add
                                ).then_inc(VE.h, 1)
                v.wait_ge(VE.h, C[("t3", c, dt)])
                if dt == 0 and c >= 1:
                    v.wait_ge(YS.h, C[("ys", c - 1)])
                v.tensor_tensor(yb8.ap()[:, dt * TCC:(dt + 1) * TCC],
                                ball.ap()[:, 0:TCC],
                                ball.ap()[:, TCC:2 * TCC], ALU.add
                                ).then_inc(VE.h, 1)

            def ve_r(v, c, dt):
                dsl = delt.ap()[:, dt * TCC:(dt + 1) * TCC]
                v.wait_ge(PE.h, C[("dpre", c, dt)])
                if c >= 1:
                    v.wait_ge(AE.h, C[("a", c - 1, dt, DS - 1)])
                v.tensor_scalar(dsl, Pm[dt % 2].ap(),
                                bdtb.ap()[:, dt:dt + 1], 0.0,
                                ALU.add, ALU.max).then_inc(VE.h, 1)

            def ve_del(v, c, dt):
                dsl = delt.ap()[:, dt * TCC:(dt + 1) * TCC]
                v.wait_ge(AE.h, C[("sp3", c, dt)])
                v.tensor_tensor(dsl, dsl, spscr[0].ap(), ALU.add
                                ).then_inc(VE.h, 1)

            def ve_m_group(v, c, dt, e6):
                if e6 is not None:
                    ve_ev_item(v, e6[0])
                    ve_ev_item(v, e6[1])
                ve_del(v, c, dt)
                ve_m_pre(v, c, dt)
                if e6 is not None:
                    ve_ev_item(v, e6[2])
                if dt < KD - 1:
                    ve_r(v, c, dt + 1)
                for n in range(0, 3):
                    ve_m_scan(v, c, dt, n)
                if e6 is not None:
                    ve_ev_item(v, e6[3])
                    ve_ev_item(v, e6[4])
                for n in range(3, DS):
                    ve_m_scan(v, c, dt, n)
                if e6 is not None:
                    ve_ev_item(v, e6[5])
                ve_m_post(v, c, dt)

            def m_dve(v):
                for slot in range(KD):
                    c = slot
                    ev = ve_ev_marks(slot)
                    if c == 0:
                        for m in ev:
                            ve_ev_item(v, m)
                    else:
                        for dt in range(KD):
                            ve_m_group(v, c - 1, dt, ev[6 * dt:6 * dt + 6])
                    for m in ve_pa_marks(slot):
                        ve_ev_item(v, m)
                    ve_r(v, c, 0)
                for dt in range(KD):
                    ve_m_group(v, NCH - 1, dt, None)
            S.emit("dve", m_dve)

            def m_sp(s):
                for c in range(NCH):
                    s.wait_ge(VE.h, C[("red", c, KD - 1)])
                    s.dma_start(
                        yscr_d.rearrange("p (j t2) -> p j t2", t2=T)[
                            :, :, c * TCC:(c + 1) * TCC],
                        yb8.ap().rearrange("p (j t2) -> p j t2", t2=TCC)
                    ).then_inc(YS.h, 16)
            S.emit("sp", m_sp)

            if DEBUG:
                DBG.bump(16)

                def dbg_b(g):
                    g.wait_ge(VE.h, EX[("pa", NE - 1, 31)])
                    g.dma_start(dacc_d, accb.ap()).then_inc(DBG.h, 16)
                    g.wait_ge(DBG.h, 48)
                S.emit("pool", dbg_b)

            S.run_block(nc)

        with ExitStack() as pt:
            sbt = lambda n_, s_, d_: pt.enter_context(nc.sbuf_tensor(n_, s_, d_))
            NTW = TCW // P
            xo = [sbt(f"xo{i}", [P, NTW * D], F32) for i in range(2)]
            ylb = [sbt(f"ylb{i}", [P, KD * TCW], BF16) for i in range(2)]
            sgf = [sbt(f"sgf{i}", [P, TCW], F32) for i in range(2)]
            Ptf2 = [pt.enter_context(nc.psum_tensor(f"Ptf2{i}", [P, P], F32))
                    for i in range(2)]

            CT = {}
            for cc in range(NCW):
                CT[("x", cc)] = SPD.bump(NTW * 16)
            for cc in range(NCW):
                CT[("y", cc)] = GPD.bump(16)
            for cc in range(NCW):
                for dt in range(KD):
                    CT[("sg", cc, dt)] = VE.bump()
                    for tau in range(NTW):
                        CT[("add", cc, dt, tau)] = VE.bump()
            for cc in range(NCW):
                for dt in range(KD):
                    for tau in range(NTW):
                        CT[("sgt", cc, dt, tau)] = PE.bump()
            for cc in range(NCW):
                CT[("out", cc)] = OUT.bump(NTW * 16)

            def t_sp(s):
                for cc in range(NCW):
                    if cc >= 2:
                        s.wait_ge(OUT.h, CT[("out", cc - 2)])
                    for tau in range(NTW):
                        tg = cc * NTW + tau
                        s.dma_start(xo[cc % 2].ap()[:, tau * D:(tau + 1) * D],
                                    x_d[tg * P:(tg + 1) * P, :]).then_inc(SPD.h, 16)
                s.wait_ge(OUT.h, CT[("out", NCW - 1)])
            S.emit("sp", t_sp)

            def t_pool(g):
                for cc in range(NCW):
                    if cc >= 2:
                        g.wait_ge(VE.h, CT[("sg", cc - 2, KD - 1)])
                    g.wait_ge(YS.h, C[("ys", (cc + 1) * (TCW // TCC) - 1)])
                    g.dma_start(
                        ylb[cc % 2].ap().rearrange("p (j t2) -> p j t2", t2=TCW),
                        yscr_d.rearrange("p (j t2) -> p j t2", t2=T)[
                            :, :, cc * TCW:(cc + 1) * TCW]).then_inc(GPD.h, 16)
            S.emit("pool", t_pool)

            def t_act(a):
                for cc in range(NCW):
                    for tau in range(NTW):
                        tg = cc * NTW + tau
                        a.wait_ge(VE.h, CT[("add", cc, KD - 1, tau)])
                        a.dma_start(out_d[tg * P:(tg + 1) * P, :],
                                    xo[cc % 2].ap()[:, tau * D:(tau + 1) * D]
                                    ).then_inc(OUT.h, 16)
            S.emit("act", t_act)

            def t_dve(v):
                for cc in range(NCW):
                    v.wait_ge(GPD.h, CT[("y", cc)])
                    for dt in range(KD):
                        if dt >= 2:
                            v.wait_ge(PE.h, CT[("sgt", cc, dt - 2, NTW - 1)])
                        elif cc >= 1:
                            v.wait_ge(PE.h, CT[("sgt", cc - 1, KD - 2 + dt, NTW - 1)])
                        v.tensor_tensor(sgf[dt % 2].ap(),
                                        ylb[cc % 2].ap()[:, dt * TCW:(dt + 1) * TCW],
                                        accb.ap()[:, dt * T + cc * TCW:
                                                  dt * T + (cc + 1) * TCW],
                                        ALU.mult).then_inc(VE.h, 1)
                        if dt == 0:
                            v.wait_ge(SPD.h, CT[("x", cc)])
                        for tau in range(NTW):
                            v.wait_ge(PE.h, CT[("sgt", cc, dt, tau)])
                            xslc = xo[cc % 2].ap()[
                                :, tau * D + dt * P:tau * D + (dt + 1) * P]
                            v.tensor_tensor(xslc,
                                            Ptf2[(dt * NTW + tau) % 2].ap(),
                                            xslc, ALU.add).then_inc(VE.h, 1)
            S.emit("dve", t_dve)

            def t_pe(t):
                t.wait_ge(AE.h, gfin)
                for cc in range(NCW):
                    for dt in range(KD):
                        t.wait_ge(VE.h, CT[("sg", cc, dt)])
                        for tau in range(NTW):
                            lin = dt * NTW + tau
                            if lin >= 2:
                                pdt, ptau = divmod(lin - 2, NTW)
                                t.wait_ge(VE.h, CT[("add", cc, pdt, ptau)])
                            elif cc >= 1:
                                t.wait_ge(VE.h,
                                          CT[("add", cc - 1, KD - 1,
                                              NTW - 2 + lin)])
                            t.transpose(Ptf2[lin % 2].ap(),
                                        sgf[dt % 2].ap()[:, tau * P:(tau + 1) * P],
                                        idf.ap()).then_inc(PE.h, 1)
            S.emit("pe", t_pe)

            S.run_block(nc)

    return nc


def _consts():
    iota = np.broadcast_to(np.arange(NE, dtype=np.float32), (P, NE)).copy()
    e16 = np.zeros((NE, NE * P), np.float32)
    for n in range(NE):
        e16[n, n * P:(n + 1) * P] = 1.0
    return {
        "identf": np.eye(P, dtype=np.float32),
        "identb": np.eye(P, dtype=ml_dtypes.bfloat16),
        "iota16": iota,
        "iota100": iota + 100.0,
        "e16": e16,
    }


def kernel(L0_attn_x, L0_attn_A_log, L0_attn_W_x, L0_attn_W_dt, L0_attn_b_dt,
           L0_attn_W_C, L0_ffn_router_w, L0_ffn_router_bias, L0_ffn_up_proj,
           L0_ffn_gate_proj, L0_ffn_down_proj, L0_ffn_sh_gate, L0_ffn_sh_up,
           L0_ffn_sh_down):
    x = np.ascontiguousarray(np.asarray(L0_attn_x, dtype=np.float32))
    shared = {
        "A_log": np.ascontiguousarray(L0_attn_A_log, dtype=np.float32),
        "W_x": np.ascontiguousarray(L0_attn_W_x, dtype=np.float32),
        "W_dt": np.ascontiguousarray(L0_attn_W_dt, dtype=np.float32),
        "b_dt": np.ascontiguousarray(L0_attn_b_dt, dtype=np.float32),
        "W_C": np.ascontiguousarray(L0_attn_W_C, dtype=np.float32),
        "router_w": np.ascontiguousarray(L0_ffn_router_w, dtype=np.float32),
        "router_bias": np.ascontiguousarray(L0_ffn_router_bias, dtype=np.float32),
        "up_proj": np.ascontiguousarray(L0_ffn_up_proj, dtype=np.float32),
        "gate_proj": np.ascontiguousarray(L0_ffn_gate_proj, dtype=np.float32),
        "down_proj": np.ascontiguousarray(L0_ffn_down_proj, dtype=np.float32),
        "sh_gate": np.ascontiguousarray(L0_ffn_sh_gate, dtype=np.float32),
        "sh_up": np.ascontiguousarray(L0_ffn_sh_up, dtype=np.float32),
        "sh_down": np.ascontiguousarray(L0_ffn_sh_down, dtype=np.float32),
        **_consts(),
    }
    nc = build_nc()
    in_maps = [dict(shared, x=np.ascontiguousarray(x[c])) for c in range(8)]
    res = run_bass_kernel_spmd(nc, in_maps, core_ids=list(range(8)))
    out = np.stack([res.results[c]["out"] for c in range(8)], axis=0)
    return out.astype(np.float32)


# version 18 (final)
# speedup vs baseline: 1.2867x; 1.0171x over previous
import numpy as np
import ml_dtypes
from contextlib import ExitStack

import concourse.bass as bass
import concourse.mybir as mybir
from concourse.bass_utils import run_bass_kernel_spmd

F32 = mybir.dt.float32
BF16 = mybir.dt.bfloat16
AF = mybir.ActivationFunctionType
ALU = mybir.AluOpType
AX = mybir.AxisListType

P = 128
D = 1024
T = 2048
KD = D // P
NTT = T // P
DS = 16
RK = 64
NE = 16
ED = 512
NET = ED // P
TCC = 256
NCH = T // TCC
TCW = 512
NCW = T // TCW

DEBUG = False


class Ctr:
    def __init__(self, stk, nc, name):
        self.h = stk.enter_context(nc.semaphore(name))
        self.v = 0

    def bump(self, k=1):
        self.v += k
        return self.v


class Sched:
    ENGS = ("pe", "act", "dve", "pool", "sp")

    def __init__(self):
        self.q = {e: [] for e in self.ENGS}

    def emit(self, eng, fn):
        self.q[eng].append(fn)

    def run_block(self, nc):
        qq = self.q
        with nc.Block() as blk:
            @blk.tensor
            def _(e):
                for f in qq["pe"]:
                    f(e)

            @blk.scalar
            def _(e):
                for f in qq["act"]:
                    f(e)

            @blk.vector
            def _(e):
                for f in qq["dve"]:
                    f(e)

            @blk.gpsimd
            def _(e):
                for f in qq["pool"]:
                    f(e)

            @blk.sync
            def _(e):
                for f in qq["sp"]:
                    f(e)
        self.q = {e: [] for e in self.ENGS}


def build_nc():
    nc = bass.Bass("TRN2", target_bir_lowering=False, detect_race_conditions=False)

    x_d = nc.dram_tensor("x", [T, D], F32, kind="ExternalInput").ap()
    alog_d = nc.dram_tensor("A_log", [D, DS], F32, kind="ExternalInput").ap()
    wx_d = nc.dram_tensor("W_x", [RK + DS, D], F32, kind="ExternalInput").ap()
    wdt_d = nc.dram_tensor("W_dt", [D, RK], F32, kind="ExternalInput").ap()
    bdt_d = nc.dram_tensor("b_dt", [D], F32, kind="ExternalInput").ap()
    wc_d = nc.dram_tensor("W_C", [DS, D], F32, kind="ExternalInput").ap()
    rw_d = nc.dram_tensor("router_w", [NE, D], F32, kind="ExternalInput").ap()
    rb_d = nc.dram_tensor("router_bias", [NE], F32, kind="ExternalInput").ap()
    up_d = nc.dram_tensor("up_proj", [NE, D, ED], F32, kind="ExternalInput").ap()
    gate_d = nc.dram_tensor("gate_proj", [NE, D, ED], F32, kind="ExternalInput").ap()
    down_d = nc.dram_tensor("down_proj", [NE, ED, D], F32, kind="ExternalInput").ap()
    shg_d = nc.dram_tensor("sh_gate", [D, D], F32, kind="ExternalInput").ap()
    shu_d = nc.dram_tensor("sh_up", [D, D], F32, kind="ExternalInput").ap()
    shd_d = nc.dram_tensor("sh_down", [D, D], F32, kind="ExternalInput").ap()
    idf_d = nc.dram_tensor("identf", [P, P], F32, kind="ExternalInput").ap()
    idb_d = nc.dram_tensor("identb", [P, P], BF16, kind="ExternalInput").ap()
    iota_d = nc.dram_tensor("iota16", [P, NE], F32, kind="ExternalInput").ap()
    iota100_d = nc.dram_tensor("iota100", [P, NE], F32, kind="ExternalInput").ap()
    e16_d = nc.dram_tensor("e16", [NE, NE * P], F32, kind="ExternalInput").ap()
    out_d = nc.dram_tensor("out", [T, D], F32, kind="ExternalOutput").ap()
    yscr_d = nc.dram_tensor("yscr", [P, KD * T], BF16, kind="Internal").ap()
    if DEBUG:
        dxn_d = nc.dram_tensor("d_xnT", [P, KD * T], F32, kind="ExternalOutput").ap()
        dcmb_d = nc.dram_tensor("d_cmb", [NE, T], F32, kind="ExternalOutput").ap()
        dacc_d = nc.dram_tensor("d_acc", [P, KD * T], F32, kind="ExternalOutput").ap()

    with ExitStack() as stk:
        sb = lambda n_, s_, d_: stk.enter_context(nc.sbuf_tensor(n_, s_, d_))
        xnT = sb("xnT", [P, KD * T], BF16)
        accb = sb("accb", [P, KD * T], BF16)
        rstdT = sb("rstdT", [P, NTT], F32)
        negmrT = sb("negmrT", [P, NTT], F32)
        Ab = sb("Ab", [P, KD * DS], F32)
        WxTb = sb("WxTb", [P, KD * (RK + DS)], BF16)
        WdtT = sb("WdtT", [RK, KD * P], BF16)
        WCT = sb("WCT", [P, KD * DS], BF16)
        bdtb = sb("bdtb", [P, KD], F32)
        idf = sb("idf", [P, P], F32)
        idb = sb("idb", [P, P], BF16)
        iota = sb("iota", [P, NE], F32)
        iota100 = sb("iota100s", [P, NE], F32)
        e16b = sb("e16b", [NE, NE * P], BF16)
        hcarry = sb("hcarry", [P, KD * DS], F32)
        cmb = sb("cmb", [NE, T], BF16)

        PE = Ctr(stk, nc, "PEs")
        AE = Ctr(stk, nc, "AEs")
        VE = Ctr(stk, nc, "VEs")
        PL = Ctr(stk, nc, "PLs")
        SPD = Ctr(stk, nc, "SPDd")
        GPD = Ctr(stk, nc, "GPDd")
        OUT = Ctr(stk, nc, "OUTd")
        YS = Ctr(stk, nc, "YSd")
        DBG = Ctr(stk, nc, "DBGd")

        S = Sched()

        with ExitStack() as psh:
            sbh = lambda n_, s_, d_: psh.enter_context(nc.sbuf_tensor(n_, s_, d_))
            shgT = sbh("shgT", [P, KD * D], BF16)
            shuT = sbh("shuT", [P, KD * D], BF16)
            shdT = sbh("shdT", [P, KD * D], BF16)

            with ExitStack() as pa:
                sba = lambda n_, s_, d_: pa.enter_context(nc.sbuf_tensor(n_, s_, d_))
                wxn = sba("wxn", [RK + DS, D], BF16)
                wdtn = sba("wdtn", [P, KD * RK], BF16)
                wcn = sba("wcn", [NE, D], BF16)
                rwn = sba("rwn", [NE, D], F32)
                alogn = sba("alogn", [P, KD * DS], F32)
                bdt8 = sba("bdt8", [KD, P], F32)
                shnat = sba("shnat", [P, KD * D], BF16)
                rbs = sba("rbs", [NE, 1], F32)
                xt = [sba(f"xt{i}", [P, D], F32) for i in range(3)]
                xsq = [sba(f"xsq{i}", [P, D], F32) for i in range(2)]
                xnbt = [sba(f"xnbt{i}", [P, D], BF16) for i in range(2)]
                xnft = [sba(f"xnft{i}", [P, D], F32) for i in range(2)]
                xfs8 = sba("xfs8", [P, D], F32)
                lgT = sba("lgT", [NE, T], F32)
                lgtm = sba("lgtm", [P, NTT * NE], F32)
                st1 = {nm: sba(nm, [P, NTT], F32)
                       for nm in ["s1c", "s2c", "mc", "qc", "vc", "m1c", "i1c",
                                  "m2c", "i2c", "w1c", "w2c", "mrc"]}
                scr16 = [sba(f"scr16_{i}", [P, NTT * NE], F32) for i in range(2)]
                ctm = sba("ctm", [P, NTT * NE], F32)

                pLog = pa.enter_context(nc.psum_tensor("pLog", [NE, T], F32))
                ptb8 = [pa.enter_context(nc.psum_tensor(f"ptb8{i}", [P, 1024], BF16))
                        for i in range(2)]
                ptf8 = pa.enter_context(nc.psum_tensor("ptf8", [P, 1024], F32))

                SPD.bump(6 * 16)
                consts_at = SPD.v

                def a1_sp(s):
                    for dst, src in [(idf, idf_d), (idb, idb_d), (iota, iota_d),
                                     (iota100, iota100_d)]:
                        s.dma_start(dst.ap(), src).then_inc(SPD.h, 16)
                    s.dma_start(rbs.ap(), rb_d.unsqueeze(1)).then_inc(SPD.h, 16)
                    s.dma_start(bdt8.ap(), bdt_d.rearrange("(c p) -> c p", p=P)
                                ).then_inc(SPD.h, 16)
                S.emit("sp", a1_sp)

                GPD.bump(7 * 16)
                w_loads = GPD.v

                def a1_gp(g):
                    g.dma_start(e16b.ap(), e16_d).then_inc(GPD.h, 16)
                    g.dma_start(wxn.ap(), wx_d).then_inc(GPD.h, 16)
                    g.dma_start(wdtn.ap().rearrange("p (c r) -> p c r", c=KD),
                                wdt_d.rearrange("(c p) r -> p c r", p=P)
                                ).then_inc(GPD.h, 16)
                    g.dma_start(wcn.ap(), wc_d).then_inc(GPD.h, 16)
                    g.dma_start(rwn.ap(), rw_d).then_inc(GPD.h, 16)
                    g.dma_start(alogn.ap().rearrange("p (c n) -> p c n", c=KD),
                                alog_d.rearrange("(c p) n -> p c n", p=P)
                                ).then_inc(GPD.h, 16)
                    g.dma_start(shnat.ap().rearrange("p (c d) -> p c d", c=KD),
                                shg_d.rearrange("(c p) d -> p c d", p=P)
                                ).then_inc(GPD.h, 16)
                S.emit("pool", a1_gp)

                rwT = sba("rwT", [P, KD * NE], F32)
                wt_groups = [
                    ("b0", [(wxn.ap()[:, j * P:(j + 1) * P], j * 80)
                            for j in range(KD)],
                     [(WxTb.ap(), P, 0, KD * 80)]),
                    ("b1", [(wdtn.ap()[:, j * RK:(j + 1) * RK], j * P)
                            for j in range(KD)],
                     [(WdtT.ap(), RK, 0, KD * P)]),
                    ("b0", [(wcn.ap()[:, j * P:(j + 1) * P], j * DS)
                            for j in range(KD)],
                     [(WCT.ap(), P, 0, KD * DS)]),
                    ("f", [(rwn.ap()[:, j * P:(j + 1) * P], j * NE)
                           for j in range(KD)] + [(bdt8.ap(), KD * NE)],
                     [(rwT.ap(), P, 0, KD * NE),
                      (bdtb.ap(), P, KD * NE, KD * NE + KD)]),
                ]
                wt_pe = {}
                wt_ve = {}
                for gi, (kind, srcs, dsts) in enumerate(wt_groups):
                    wt_pe[gi] = PE.bump(len(srcs))
                for gi, (kind, srcs, dsts) in enumerate(wt_groups):
                    wt_ve[gi] = VE.bump(len(dsts))

                def wt_ps(kind):
                    return {"b0": ptb8[0], "b1": ptb8[1], "f": ptf8}[kind]

                def a2_pe(t):
                    t.wait_ge(GPD.h, w_loads)
                    t.wait_ge(SPD.h, consts_at)
                    seen = {}
                    for gi, (kind, srcs, dsts) in enumerate(wt_groups):
                        if kind in seen:
                            t.wait_ge(VE.h, wt_ve[seen[kind]])
                        seen[kind] = gi
                        ps = wt_ps(kind)
                        for sap, coff in srcs:
                            npart = sap.shape[0]
                            ident = idb if kind != "f" else idf
                            t.transpose(ps.ap()[0:sap.shape[1], coff:coff + npart],
                                        sap, ident.ap()[0:npart, 0:npart]
                                        ).then_inc(PE.h, 1)
                S.emit("pe", a2_pe)

                def a2_dve(v):
                    for gi, (kind, srcs, dsts) in enumerate(wt_groups):
                        ps = wt_ps(kind)
                        v.wait_ge(PE.h, wt_pe[gi])
                        for dap, rows, lo, hi in dsts:
                            v.tensor_copy(dap, ps.ap()[0:rows, lo:hi]
                                          ).then_inc(VE.h, 1)
                S.emit("dve", a2_dve)

                aexp_at = AE.bump()

                def a2_act(a):
                    a.wait_ge(GPD.h, w_loads)
                    a.activation(Ab.ap(), alogn.ap(), AF.Exp).then_inc(AE.h, 1)
                S.emit("act", a2_act)

                aneg_at = VE.bump()

                def a2_dve2(v):
                    v.wait_ge(AE.h, aexp_at)
                    v.tensor_scalar(Ab.ap(), Ab.ap(), -1.0, None, ALU.mult
                                    ).then_inc(VE.h, 1)
                S.emit("dve", a2_dve2)

                A3 = {tau: {} for tau in range(NTT)}
                for tau in range(NTT):
                    A3[tau]["x"] = SPD.bump(16)
                for tau in range(NTT):
                    d = A3[tau]
                    d["sq"] = AE.bump()
                    d["ln"] = AE.bump()
                    d["rstd"] = AE.bump()
                    d["xnb"] = AE.bump()
                    d["xnf"] = AE.bump()
                for tau in range(NTT):
                    d = A3[tau]
                    for nm in ["s1", "s2", "m", "q", "v1", "v2", "stats",
                               "mr1", "mr", "bevac", "fevac"]:
                        d[nm] = VE.bump()
                for tau in range(NTT):
                    d = A3[tau]
                    d["btr"] = PE.bump(KD)
                    d["ftr"] = PE.bump(KD)
                    d["lmm"] = [PE.bump() for _ in range(KD)]

                def a3_sp(s):
                    for tau in range(NTT):
                        if tau >= 3:
                            s.wait_ge(AE.h, A3[tau - 3]["xnf"])
                            s.wait_ge(VE.h, A3[tau - 3]["stats"])
                        s.dma_start(xt[tau % 3].ap(), x_d[tau * P:(tau + 1) * P, :]
                                    ).then_inc(SPD.h, 16)
                S.emit("sp", a3_sp)

                def a3_act(a):
                    for tau in range(NTT):
                        xa = xt[tau % 3].ap()
                        a.wait_ge(SPD.h, A3[tau]["x"])
                        if tau >= 2:
                            a.wait_ge(VE.h, A3[tau - 2]["s2"])
                        a.activation(xsq[tau % 2].ap(), xa, AF.Square
                                     ).then_inc(AE.h, 1)
                        a.wait_ge(VE.h, A3[tau]["stats"])
                        vcol = st1["vc"].ap()[:, tau:tau + 1]
                        a.activation(st1["qc"].ap()[:, tau:tau + 1], vcol, AF.Ln
                                     ).then_inc(AE.h, 1)
                        a.wait_ge(AE.h, A3[tau]["ln"])
                        a.activation(rstdT.ap()[:, tau:tau + 1],
                                     st1["qc"].ap()[:, tau:tau + 1], AF.Exp,
                                     scale=-0.5).then_inc(AE.h, 1)
                        a.wait_ge(VE.h, A3[tau]["mr"])
                        if tau >= 2:
                            a.wait_ge(PE.h, A3[tau - 2]["btr"])
                        if tau >= 2:
                            a.wait_ge(PE.h, A3[tau - 2]["ftr"])
                        a.activation(xnbt[tau % 2].ap(), xa, AF.Identity,
                                     bias=negmrT.ap()[:, tau:tau + 1],
                                     scale=rstdT.ap()[:, tau:tau + 1]
                                     ).then_inc(AE.h, 1)
                        a.activation(xnft[tau % 2].ap(), xa, AF.Identity,
                                     bias=negmrT.ap()[:, tau:tau + 1],
                                     scale=rstdT.ap()[:, tau:tau + 1]
                                     ).then_inc(AE.h, 1)
                S.emit("act", a3_act)

                def a3_dve(v):
                    inv = 1.0 / D
                    for tau in range(NTT):
                        xa = xt[tau % 3].ap()
                        tc = lambda n: st1[n].ap()[:, tau:tau + 1]
                        d = A3[tau]
                        v.wait_ge(SPD.h, d["x"])
                        v.reduce_sum(tc("s1c"), xa, axis=AX.X).then_inc(VE.h, 1)
                        v.wait_ge(AE.h, d["sq"])
                        v.reduce_sum(tc("s2c"), xsq[tau % 2].ap(), axis=AX.X
                                     ).then_inc(VE.h, 1)
                        v.wait_ge(VE.h, d["s1"])
                        v.tensor_scalar(tc("mc"), tc("s1c"), inv, None, ALU.mult
                                        ).then_inc(VE.h, 1)
                        v.wait_ge(VE.h, d["s2"])
                        v.tensor_scalar(tc("qc"), tc("s2c"), inv, None, ALU.mult
                                        ).then_inc(VE.h, 1)
                        v.wait_ge(VE.h, d["m"])
                        v.tensor_tensor(tc("vc"), tc("mc"), tc("mc"), ALU.mult
                                        ).then_inc(VE.h, 1)
                        v.wait_ge(VE.h, max(d["q"], d["v1"]))
                        v.tensor_tensor(tc("vc"), tc("qc"), tc("vc"), ALU.subtract
                                        ).then_inc(VE.h, 1)
                        v.wait_ge(VE.h, d["v2"])
                        v.tensor_scalar(tc("vc"), tc("vc"), 1e-5, None, ALU.add
                                        ).then_inc(VE.h, 1)
                        v.wait_ge(AE.h, d["rstd"])
                        v.tensor_tensor(tc("mrc"), tc("mc"),
                                        rstdT.ap()[:, tau:tau + 1], ALU.mult
                                        ).then_inc(VE.h, 1)
                        v.wait_ge(VE.h, d["mr1"])
                        v.tensor_scalar(negmrT.ap()[:, tau:tau + 1], tc("mrc"),
                                        -1.0, None, ALU.mult).then_inc(VE.h, 1)
                        v.wait_ge(PE.h, d["btr"])
                        v.tensor_copy(
                            xnT.ap().rearrange("p (j t2) -> p j t2", t2=T)[
                                :, :, tau * P:(tau + 1) * P],
                            ptb8[tau % 2].ap().rearrange("p (j t2) -> p j t2", t2=P)
                        ).then_inc(VE.h, 1)
                        v.wait_ge(PE.h, d["ftr"])
                        if tau >= 1:
                            v.wait_ge(PE.h, A3[tau - 1]["lmm"][KD - 1])
                        v.tensor_copy(xfs8.ap(), ptf8.ap()).then_inc(VE.h, 1)
                S.emit("dve", a3_dve)

                def a3_pe(t):
                    for tau in range(NTT):
                        d = A3[tau]
                        t.wait_ge(AE.h, d["xnb"])
                        if tau >= 2:
                            t.wait_ge(VE.h, A3[tau - 2]["bevac"])
                        for j in range(KD):
                            t.transpose(ptb8[tau % 2].ap()[:, j * P:(j + 1) * P],
                                        xnbt[tau % 2].ap()[:, j * P:(j + 1) * P],
                                        idb.ap()).then_inc(PE.h, 1)
                        t.wait_ge(AE.h, d["xnf"])
                        if tau >= 1:
                            t.wait_ge(VE.h, A3[tau - 1]["fevac"])
                        for j in range(KD):
                            t.transpose(ptf8.ap()[:, j * P:(j + 1) * P],
                                        xnft[tau % 2].ap()[:, j * P:(j + 1) * P],
                                        idf.ap()).then_inc(PE.h, 1)
                        t.wait_ge(VE.h, d["fevac"])
                        for j in range(KD):
                            t.matmul(pLog.ap()[:, tau * P:(tau + 1) * P],
                                     rwT.ap()[:, j * NE:(j + 1) * NE],
                                     xfs8.ap()[:, j * P:(j + 1) * P],
                                     start=(j == 0), stop=(j == KD - 1)
                                     ).then_inc(PE.h, 1)
                S.emit("pe", a3_pe)

                lgb_at = AE.bump()

                def a4_act(a):
                    a.wait_ge(PE.h, A3[NTT - 1]["lmm"][KD - 1])
                    a.activation(lgT.ap(), pLog.ap(), AF.Identity, bias=rbs.ap()
                                 ).then_inc(AE.h, 1)
                S.emit("act", a4_act)

                lgtr = PE.bump(NTT)
                lgev = VE.bump()

                def a4_pe(t):
                    t.wait_ge(AE.h, lgb_at)
                    for tau in range(NTT):
                        t.transpose(ptf8.ap()[:, tau * NE:(tau + 1) * NE],
                                    lgT.ap()[:, tau * P:(tau + 1) * P],
                                    idf.ap()[0:NE, 0:NE]).then_inc(PE.h, 1)
                S.emit("pe", a4_pe)

                def a4_dve(v):
                    v.wait_ge(PE.h, lgtr)
                    v.tensor_copy(lgtm.ap(), ptf8.ap()[:, 0:NTT * NE]
                                  ).then_inc(VE.h, 1)
                S.emit("dve", a4_dve)

                lgs = lgtm.ap()

                def batch_step(prereqs, opgen):
                    at = VE.bump()

                    def f(v):
                        for sh, val in prereqs:
                            v.wait_ge(sh, val)
                        last = None
                        for tau in range(NTT):
                            last = opgen(v, tau)
                        last.then_inc(VE.h, 1)
                    S.emit("dve", f)
                    return at

                sl = lambda t_, tau: t_[:, tau * NE:(tau + 1) * NE]
                col = lambda n_, tau: st1[n_].ap()[:, tau:tau + 1]

                m1_at = batch_step([(VE.h, lgev)],
                                   lambda v, tau: v.reduce_max(
                                       col("m1c", tau), sl(lgs, tau), axis=AX.X))
                eq1_at = batch_step([(VE.h, m1_at)],
                                    lambda v, tau: v.tensor_single_scalar(
                                        sl(scr16[0].ap(), tau), sl(lgs, tau),
                                        col("m1c", tau), ALU.is_equal))
                cand1_at = batch_step([(VE.h, eq1_at)],
                                      lambda v, tau: v.scalar_tensor_tensor(
                                          sl(scr16[1].ap(), tau),
                                          sl(scr16[0].ap(), tau),
                                          -100.0, iota100.ap(), ALU.mult, ALU.add))
                i1_at = batch_step([(VE.h, cand1_at)],
                                   lambda v, tau: v.tensor_reduce(
                                       col("i1c", tau), sl(scr16[1].ap(), tau),
                                       AX.X, ALU.min))
                mask1_at = batch_step([(VE.h, i1_at)],
                                      lambda v, tau: v.tensor_single_scalar(
                                          sl(scr16[0].ap(), tau), iota.ap(),
                                          col("i1c", tau), ALU.is_equal))
                l2_at = batch_step([(VE.h, mask1_at)],
                                   lambda v, tau: v.scalar_tensor_tensor(
                                       sl(scr16[1].ap(), tau),
                                       sl(scr16[0].ap(), tau),
                                       -1.0e30, sl(lgs, tau), ALU.mult, ALU.add))
                m2_at = batch_step([(VE.h, l2_at)],
                                   lambda v, tau: v.reduce_max(
                                       col("m2c", tau), sl(scr16[1].ap(), tau),
                                       axis=AX.X))
                eq2_at = batch_step([(VE.h, m2_at)],
                                    lambda v, tau: v.tensor_single_scalar(
                                        sl(ctm.ap(), tau), sl(scr16[1].ap(), tau),
                                        col("m2c", tau), ALU.is_equal))
                cand2_at = batch_step([(VE.h, eq2_at)],
                                      lambda v, tau: v.scalar_tensor_tensor(
                                          sl(ctm.ap(), tau), sl(ctm.ap(), tau),
                                          -100.0, iota100.ap(), ALU.mult, ALU.add))
                i2_at = batch_step([(VE.h, cand2_at)],
                                   lambda v, tau: v.tensor_reduce(
                                       col("i2c", tau), sl(ctm.ap(), tau),
                                       AX.X, ALU.min))

                dm_at = VE.bump()

                def s_dm(v):
                    v.wait_ge(VE.h, i2_at)
                    v.tensor_sub(st1["w1c"].ap(), st1["m1c"].ap(),
                                 st1["m2c"].ap()).then_inc(VE.h, 1)
                S.emit("dve", s_dm)

                sig_at = AE.bump(2)

                def s_sig(a):
                    a.wait_ge(VE.h, dm_at)
                    a.activation(st1["w2c"].ap(), st1["w1c"].ap(), AF.Sigmoid,
                                 scale=-1.0).then_inc(AE.h, 1)
                    a.activation(st1["w1c"].ap(), st1["w1c"].ap(), AF.Sigmoid
                                 ).then_inc(AE.h, 1)
                S.emit("act", s_sig)

                mask2_at = batch_step([(AE.h, sig_at)],
                                      lambda v, tau: v.tensor_single_scalar(
                                          sl(scr16[1].ap(), tau), iota.ap(),
                                          col("i2c", tau), ALU.is_equal))
                cw_at = VE.bump()

                def s_cw(v):
                    v.wait_ge(VE.h, mask2_at)
                    for tau in range(NTT):
                        v.tensor_scalar(sl(ctm.ap(), tau), sl(scr16[0].ap(), tau),
                                        col("w1c", tau), None, ALU.mult)
                    last = None
                    for tau in range(NTT):
                        last = v.tensor_scalar(sl(scr16[1].ap(), tau),
                                               sl(scr16[1].ap(), tau),
                                               col("w2c", tau), None, ALU.mult)
                    last.then_inc(VE.h, 1)
                S.emit("dve", s_cw)

                comb_at = batch_step([(VE.h, cw_at)],
                                     lambda v, tau: v.tensor_add(
                                         sl(ctm.ap(), tau), sl(ctm.ap(), tau),
                                         sl(scr16[1].ap(), tau)))

                ctr_at = PE.bump(NTT)
                cev_at = VE.bump()

                def a4_pe2(t):
                    t.wait_ge(VE.h, comb_at)
                    for tau in range(NTT):
                        t.transpose(pLog.ap()[0:NE, tau * P:(tau + 1) * P],
                                    ctm.ap()[:, tau * NE:(tau + 1) * NE],
                                    idf.ap()).then_inc(PE.h, 1)
                S.emit("pe", a4_pe2)

                def a4_dve2(v):
                    v.wait_ge(PE.h, ctr_at)
                    v.tensor_copy(cmb.ap(), pLog.ap()[0:NE, :]).then_inc(VE.h, 1)
                S.emit("dve", a4_dve2)

                sh_pe = {}
                sh_ve = {}
                for w in range(3):
                    for dc in range(KD):
                        sh_pe[(w, dc)] = PE.bump(KD)
                for w in range(3):
                    for dc in range(KD):
                        sh_ve[(w, dc)] = VE.bump()
                shu_at = GPD.bump(16)
                shd_at = GPD.bump(16)

                def a5_gp(g):
                    g.wait_ge(PE.h, sh_pe[(0, KD - 1)])
                    g.dma_start(shnat.ap().rearrange("p (c d) -> p c d", c=KD),
                                shu_d.rearrange("(c p) d -> p c d", p=P)
                                ).then_inc(GPD.h, 16)
                    g.wait_ge(PE.h, sh_pe[(1, KD - 1)])
                    g.dma_start(shnat.ap().rearrange("p (c d) -> p c d", c=KD),
                                shd_d.rearrange("(c p) d -> p c d", p=P)
                                ).then_inc(GPD.h, 16)
                S.emit("pool", a5_gp)

                def a5_pe(t):
                    for w, dstT in enumerate([shgT, shuT, shdT]):
                        if w == 1:
                            t.wait_ge(GPD.h, shu_at)
                        if w == 2:
                            t.wait_ge(GPD.h, shd_at)
                        for dc in range(KD):
                            lin = w * KD + dc
                            if lin >= 2:
                                pw, pdc = divmod(lin - 2, KD)
                                t.wait_ge(VE.h, sh_ve[(pw, pdc)])
                            for hc in range(KD):
                                t.transpose(
                                    ptb8[dc % 2].ap()[:, hc * P:(hc + 1) * P],
                                    shnat.ap()[:, hc * D + dc * P:
                                               hc * D + (dc + 1) * P],
                                    idb.ap()).then_inc(PE.h, 1)
                S.emit("pe", a5_pe)

                def a5_dve(v):
                    for w, dstT in enumerate([shgT, shuT, shdT]):
                        for dc in range(KD):
                            v.wait_ge(PE.h, sh_pe[(w, dc)])
                            v.tensor_copy(dstT.ap()[:, dc * D:(dc + 1) * D],
                                          ptb8[dc % 2].ap()).then_inc(VE.h, 1)
                S.emit("dve", a5_dve)

                if DEBUG:
                    DBG.bump(32)
                    vnow = VE.v

                    def dbg_a(g):
                        g.wait_ge(VE.h, vnow)
                        g.dma_start(dcmb_d, cmb.ap()).then_inc(DBG.h, 16)
                        g.dma_start(dxn_d, xnT.ap()).then_inc(DBG.h, 16)
                        g.wait_ge(DBG.h, 32)
                    S.emit("pool", dbg_a)

                S.run_block(nc)

            shh = sbh("shh", [P, KD * T], BF16)
            sigbg = [sbh(f"sigbg{i}", [P, 1024], BF16) for i in range(2)]
            gPA = psh.enter_context(nc.psum_tensor("gPA", [P, 1024], F32))
            gPB = psh.enter_context(nc.psum_tensor("gPB", [P, 1024], F32))
            gPC = [psh.enter_context(nc.psum_tensor(f"gPC{i}", [P, 512], F32))
                   for i in range(2)]

            glu = {}
            for th in range(2):
                for ht in range(KD):
                    glu[("g", th, ht)] = PE.bump(16)
                    glu[("u", th, ht)] = PE.bump(16)
                for dt in range(KD):
                    for q in range(2):
                        glu[("d", th, dt, q)] = PE.bump(KD)
            for th in range(2):
                for ht in range(KD):
                    glu[("sig", th, ht)] = AE.bump()
            for th in range(2):
                for ht in range(KD):
                    glu[("shh", th, ht)] = VE.bump()
                for dt in range(KD):
                    for q in range(2):
                        glu[("acc", th, dt, q)] = VE.bump()

            def prev_ht(th, ht, k):
                lin = th * KD + ht - k
                return (lin // KD, lin % KD) if lin >= 0 else None

            def b1_pe(t):
                for th in range(2):
                    for ht in range(KD):
                        pv = prev_ht(th, ht, 1)
                        if pv is not None:
                            t.wait_ge(AE.h, glu[("sig",) + pv])
                        for dc in range(KD):
                            for q in range(2):
                                t.matmul(gPA.ap()[:, q * 512:(q + 1) * 512],
                                         shgT.ap()[:, dc * D + ht * P:
                                                   dc * D + (ht + 1) * P],
                                         xnT.ap()[:, dc * T + th * 1024 + q * 512:
                                                  dc * T + th * 1024 + (q + 1) * 512],
                                         start=(dc == 0), stop=(dc == KD - 1)
                                         ).then_inc(PE.h, 1)
                        if pv is not None:
                            t.wait_ge(VE.h, glu[("shh",) + pv])
                        for dc in range(KD):
                            for q in range(2):
                                t.matmul(gPB.ap()[:, q * 512:(q + 1) * 512],
                                         shuT.ap()[:, dc * D + ht * P:
                                                   dc * D + (ht + 1) * P],
                                         xnT.ap()[:, dc * T + th * 1024 + q * 512:
                                                  dc * T + th * 1024 + (q + 1) * 512],
                                         start=(dc == 0), stop=(dc == KD - 1)
                                         ).then_inc(PE.h, 1)
                    t.wait_ge(VE.h, glu[("shh", th, KD - 1)])
                    for dt in range(KD):
                        for q in range(2):
                            lin = (th * KD + dt) * 2 + q
                            if lin >= 2:
                                plin = lin - 2
                                pth = plin // (2 * KD)
                                pr = plin % (2 * KD)
                                t.wait_ge(VE.h, glu[("acc", pth, pr // 2, pr % 2)])
                            for hc in range(KD):
                                t.matmul(gPC[lin % 2].ap(),
                                         shdT.ap()[:, hc * D + dt * P:
                                                   hc * D + (dt + 1) * P],
                                         shh.ap()[:, hc * T + th * 1024 + q * 512:
                                                  hc * T + th * 1024 + (q + 1) * 512],
                                         start=(hc == 0), stop=(hc == KD - 1)
                                         ).then_inc(PE.h, 1)
            S.emit("pe", b1_pe)

            def b1_act(a):
                for th in range(2):
                    for ht in range(KD):
                        a.wait_ge(PE.h, glu[("g", th, ht)])
                        pv = prev_ht(th, ht, 2)
                        if pv is not None:
                            a.wait_ge(VE.h, glu[("shh",) + pv])
                        a.activation(sigbg[ht % 2].ap(), gPA.ap(), AF.Silu
                                     ).then_inc(AE.h, 1)
            S.emit("act", b1_act)

            def b1_dve(v):
                for th in range(2):
                    for ht in range(KD):
                        v.wait_ge(PE.h, glu[("u", th, ht)])
                        v.wait_ge(AE.h, glu[("sig", th, ht)])
                        v.tensor_tensor(shh.ap()[:, ht * T + th * 1024:
                                                 ht * T + (th + 1) * 1024],
                                        gPB.ap(), sigbg[ht % 2].ap(), ALU.mult
                                        ).then_inc(VE.h, 1)
                    for dt in range(KD):
                        for q in range(2):
                            lin = (th * KD + dt) * 2 + q
                            v.wait_ge(PE.h, glu[("d", th, dt, q)])
                            v.tensor_copy(
                                accb.ap()[:, dt * T + th * 1024 + q * 512:
                                          dt * T + th * 1024 + (q + 1) * 512],
                                gPC[lin % 2].ap()).then_inc(VE.h, 1)
            S.emit("dve", b1_dve)

            S.run_block(nc)

        with ExitStack() as pm3:
            sbm = lambda n_, s_, d_: pm3.enter_context(nc.sbuf_tensor(n_, s_, d_))
            delt = sbm("delt", [P, KD * TCC], F32)
            Bball = sbm("Bball", [P, DS * TCC], BF16)
            Cball = sbm("Cball", [P, DS * TCC], BF16)
            ball = sbm("ball", [P, DS * TCC], BF16)
            yb8 = sbm("yb8", [P, KD * TCC], BF16)
            Btf = sbm("Btf", [NE, TCC], BF16)
            Ctf = sbm("Ctf", [NE, TCC], BF16)
            dlowb = sbm("dlowb", [RK, TCC], BF16)
            ub1 = sbm("ub1", [P, TCC], BF16)
            spscr = [sbm(f"spscr{i}", [P, TCC], F32) for i in range(2)]
            abuf = [sbm(f"abuf{i}", [P, TCC], F32) for i in range(6)]
            upw = [sbm(f"upw{i}", [P, KD * ED], BF16) for i in range(2)]
            gatew = [sbm("gatew0", [P, KD * ED], BF16)]
            downw = [sbm(f"downw{i}", [P, NET * D], BF16) for i in range(2)]
            hidp = [sbm(f"hidp{i}", [P, NET * T], BF16) for i in range(2)]
            cbf = sbm("cbf", [P, T], BF16)
            sigb2 = [sbm(f"sigb2{i}", [P, 1024], BF16) for i in range(2)]
            upcb = [sbm(f"upcb{i}", [P, 1024], BF16) for i in range(2)]

            U = [pm3.enter_context(nc.psum_tensor(f"U{i}", [P, 512], F32))
                 for i in range(4)]
            PB = pm3.enter_context(nc.psum_tensor("PB", [P, 1024], F32))
            Pm = [pm3.enter_context(nc.psum_tensor(f"Pm{i}", [P, TCC], F32))
                  for i in range(2)]

            hz_at = VE.bump()

            def m_init(v):
                v.memset(hcarry.ap(), 0.0).then_inc(VE.h, 1)
                v.wait_ge(VE.h, hz_at)
            S.emit("dve", m_init)

            EX = {}
            C = {}
            for k in range(NE):
                EX[("wug", k)] = GPD.bump(32)
                EX[("wdn", k)] = GPD.bump(16)

            useq = []

            def pe_expert_marks(k, dpre_c=None, ki=0):
                for h in range(4):
                    EX[("cb", k, h)] = PE.bump()
                    useq.append(("AE", ("cbe", k, h)))
                for s in range(KD):
                    for q in range(2):
                        EX[("up", k, s, q)] = PE.bump(KD)
                        useq.append(("VE", ("upch", k, s, q)))
                    EX[("gate", k, s)] = PE.bump(16)
                    if dpre_c is not None and s % 2 == 1:
                        j = ki * 4 + (s - 1) // 2
                        if j <= 5:
                            C[("dpre", dpre_c, j + 2)] = PE.bump()

            for slot in range(KD):
                c = slot
                dc = c - 1 if c >= 1 else None
                pe_expert_marks(2 * slot, dpre_c=dc, ki=0)
                pe_expert_marks(2 * slot + 1, dpre_c=dc, ki=1)
                kk = 2 * slot + 1

                def pd_mark(lin):
                    EX[("pd", kk, lin)] = PE.bump(KD)
                    useq.append(("VE", ("pa", kk, lin)))

                C[("bcB", c)] = PE.bump(KD)
                C[("bcC", c)] = PE.bump(KD)
                C[("dlow", c)] = PE.bump(KD)
                for lin in range(8):
                    pd_mark(lin)
                for i in range(KD):
                    C[("bbm", c, 2 * i)] = PE.bump()
                    C[("bbm", c, 2 * i + 1)] = PE.bump()
                    pd_mark(8 + i)
                for i in range(KD):
                    C[("cbm", c, 2 * i)] = PE.bump()
                    C[("cbm", c, 2 * i + 1)] = PE.bump()
                    pd_mark(16 + i)
                for dt in range(2):
                    C[("dpre", c, dt)] = PE.bump()
                for lin in range(24, 32):
                    pd_mark(lin)
            for dt in range(2, KD):
                C[("dpre", NCH - 1, dt)] = PE.bump()

            def ae_sp_marks(c, dt):
                C[("r", c, dt)] = AE.bump()
                C[("sp1", c, dt)] = AE.bump()
                C[("sp2", c, dt)] = AE.bump()
                C[("sp3", c, dt)] = AE.bump()

            def ae_a_marks(c, dt):
                for n in range(DS):
                    C[("a", c, dt, n)] = AE.bump()

            def ae_ga_marks(c, dt):
                for n in range(6):
                    C[("a", c, dt, n)] = AE.bump()
                if dt < KD - 1:
                    C[("r", c, dt + 1)] = AE.bump()
                    C[("sp1", c, dt + 1)] = AE.bump()
                C[("a", c, dt, 6)] = AE.bump()
                if dt < KD - 1:
                    C[("sp2", c, dt + 1)] = AE.bump()
                C[("a", c, dt, 7)] = AE.bump()
                if dt < KD - 1:
                    C[("sp3", c, dt + 1)] = AE.bump()
                for n in range(8, DS):
                    C[("a", c, dt, n)] = AE.bump()

            for slot in range(KD):
                c = slot
                for ki, k in enumerate((2 * slot, 2 * slot + 1)):
                    for h in range(4):
                        EX[("cbe", k, h)] = AE.bump()
                    for s in range(KD):
                        EX[("sig", k, s)] = AE.bump()
                        if c >= 1 and s % 2 == 0:
                            ae_ga_marks(c - 1, ki * 4 + s // 2)
                C[("Bev", c)] = AE.bump()
                C[("Cev", c)] = AE.bump()
                C[("dlev", c)] = AE.bump()
                for n in range(DS):
                    C[("bbe", c, n)] = AE.bump()
                for n in range(DS):
                    C[("cbe", c, n)] = AE.bump()
                ae_sp_marks(c, 0)
            for dt in range(KD):
                ae_ga_marks(NCH - 1, dt)
            gfin = AE.bump(KD)

            def ve_group_marks(c, dt, e6):
                if e6 is not None:
                    EX[e6[0]] = VE.bump()
                    EX[e6[1]] = VE.bump()
                C[("del", c, dt)] = VE.bump()
                C[("u", c, dt)] = VE.bump()
                C[("ball", c, dt)] = VE.bump()
                if e6 is not None:
                    EX[e6[2]] = VE.bump()
                for n in range(0, 3):
                    C[("scan", c, dt, n)] = VE.bump()
                if e6 is not None:
                    EX[e6[3]] = VE.bump()
                    EX[e6[4]] = VE.bump()
                for n in range(3, DS):
                    C[("scan", c, dt, n)] = VE.bump()
                if e6 is not None:
                    EX[e6[5]] = VE.bump()
                C[("carry", c, dt)] = VE.bump()
                C[("M", c, dt)] = VE.bump()
                C[("t1", c, dt)] = VE.bump()
                C[("t2", c, dt)] = VE.bump()
                C[("t3", c, dt)] = VE.bump()
                C[("red", c, dt)] = VE.bump()

            def ve_ev_marks(slot):
                marks = []
                for k in (2 * slot, 2 * slot + 1):
                    for s in range(KD):
                        marks.append(("upch", k, s, 0))
                        marks.append(("upch", k, s, 1))
                        marks.append(("hid", k, s))
                return marks

            def ve_pa_marks(slot):
                kk = 2 * slot + 1
                return [("pa", kk, lin) for lin in range(32)]

            for slot in range(KD):
                c = slot
                ev = ve_ev_marks(slot)
                if c == 0:
                    for m in ev:
                        EX[m] = VE.bump()
                else:
                    for dt in range(KD):
                        ve_group_marks(c - 1, dt, ev[6 * dt:6 * dt + 6])
                for m in ve_pa_marks(slot):
                    EX[m] = VE.bump()
            for dt in range(KD):
                ve_group_marks(NCH - 1, dt, None)
            for c in range(NCH):
                C[("ys", c)] = YS.bump(16)

            def pdlast(k):
                o = k - 1 if k % 2 == 0 else k - 2
                return EX[("pd", o, 31)] if o >= 1 else None

            def m_pool(g):
                for k in range(NE):
                    b = k % 2
                    if k >= 2:
                        g.wait_ge(PE.h, EX[("gate", k - 2, KD - 1)])
                    g.dma_start(upw[b].ap().rearrange("p (c e) -> p c e", c=KD),
                                up_d[k].rearrange("(c p) e -> p c e", p=P)
                                ).then_inc(GPD.h, 16)
                    if k >= 1:
                        g.wait_ge(PE.h, EX[("gate", k - 1, KD - 1)])
                    g.dma_start(gatew[0].ap().rearrange("p (c e) -> p c e", c=KD),
                                gate_d[k].rearrange("(c p) e -> p c e", p=P)
                                ).then_inc(GPD.h, 16)
                    if k >= 2:
                        g.wait_ge(PE.h, pdlast(k))
                    g.dma_start(downw[b].ap().rearrange("p (c d) -> p c d", c=NET),
                                down_d[k].rearrange("(c p) d -> p c d", p=P)
                                ).then_inc(GPD.h, 16)
            S.emit("pool", m_pool)

            uidx = [0]

            def u_wait(t):
                i = uidx[0]
                if i >= 4:
                    eng, key = useq[i - 4]
                    if eng == "AE":
                        t.wait_ge(AE.h, EX[key])
                    else:
                        t.wait_ge(VE.h, EX[key])
                uidx[0] += 1
                return U[i % 4]

            def pe_expert(t, k, dpre_c=None, ki=0):
                b = k % 2
                t.wait_ge(GPD.h, EX[("wug", k)])
                for h in range(4):
                    th, q = divmod(h, 2)
                    ut = u_wait(t)
                    t.matmul(ut.ap(),
                             e16b.ap()[:, k * P:(k + 1) * P],
                             cmb.ap()[:, th * 1024 + q * 512:
                                      th * 1024 + (q + 1) * 512],
                             start=True, stop=True).then_inc(PE.h, 1)
                for s in range(KD):
                    et, th = s // 2, s % 2
                    for q in range(2):
                        ut = u_wait(t)
                        for kc in range(KD):
                            t.matmul(
                                ut.ap(),
                                upw[b].ap()[:, kc * ED + et * P:
                                            kc * ED + (et + 1) * P],
                                xnT.ap()[:, kc * T + th * 1024 + q * 512:
                                         kc * T + th * 1024 + (q + 1) * 512],
                                start=(kc == 0), stop=(kc == KD - 1)
                            ).then_inc(PE.h, 1)
                    if s >= 1:
                        t.wait_ge(AE.h, EX[("sig", k, s - 1)])
                    elif k >= 1:
                        t.wait_ge(AE.h, EX[("sig", k - 1, KD - 1)])
                    for kc in range(KD):
                        for q in range(2):
                            t.matmul(
                                PB.ap()[:, q * 512:(q + 1) * 512],
                                gatew[0].ap()[:, kc * ED + et * P:
                                              kc * ED + (et + 1) * P],
                                xnT.ap()[:, kc * T + th * 1024 + q * 512:
                                         kc * T + th * 1024 + (q + 1) * 512],
                                start=(kc == 0), stop=(kc == KD - 1)
                            ).then_inc(PE.h, 1)
                    if dpre_c is not None and s % 2 == 1:
                        j = ki * 4 + (s - 1) // 2
                        if j <= 5:
                            pe_dpre(t, dpre_c, j + 2)

            def pe_dpre(t, c, dt):
                if dt < 2:
                    t.wait_ge(AE.h, C[("cbe", c, DS - 2 + dt)])
                else:
                    t.wait_ge(AE.h, C[("sp1", c, dt - 2)])
                t.matmul(Pm[dt % 2].ap(),
                         WdtT.ap()[:, dt * P:(dt + 1) * P], dlowb.ap(),
                         start=True, stop=True).then_inc(PE.h, 1)

            def m_pe(t):
                for slot in range(KD):
                    c = slot
                    dc = c - 1 if c >= 1 else None
                    pe_expert(t, 2 * slot, dpre_c=dc, ki=0)
                    pe_expert(t, 2 * slot + 1, dpre_c=dc, ki=1)
                    kk = 2 * slot + 1

                    def pe_down_lin(lin):
                        dt, r = divmod(lin, 4)
                        th, q = divmod(r, 2)
                        if lin == 0:
                            t.wait_ge(VE.h, EX[("hid", kk, KD - 1)])
                        ut = u_wait(t)
                        first = True
                        for half, ke in ((0, kk - 1), (1, kk)):
                            for ec in range(NET):
                                t.matmul(
                                    ut.ap(),
                                    downw[ke % 2].ap()[:, ec * D + dt * P:
                                                       ec * D + (dt + 1) * P],
                                    hidp[ke % 2].ap()[
                                        :, ec * T + th * 1024 + q * 512:
                                        ec * T + th * 1024 + (q + 1) * 512],
                                    start=first,
                                    stop=(half == 1 and ec == NET - 1)
                                ).then_inc(PE.h, 1)
                                first = False

                    if c >= 1:
                        t.wait_ge(AE.h, C[("sp1", c - 1, KD - 1)])
                    for kc in range(KD):
                        t.matmul(Pm[0].ap()[0:NE, :],
                                 WxTb.ap()[:, kc * 80 + RK:kc * 80 + 80],
                                 xnT.ap()[:, kc * T + c * TCC:kc * T + (c + 1) * TCC],
                                 start=(kc == 0), stop=(kc == KD - 1)
                                 ).then_inc(PE.h, 1)
                    for kc in range(KD):
                        t.matmul(Pm[1].ap()[0:NE, :],
                                 WCT.ap()[:, kc * DS:(kc + 1) * DS],
                                 xnT.ap()[:, kc * T + c * TCC:kc * T + (c + 1) * TCC],
                                 start=(kc == 0), stop=(kc == KD - 1)
                                 ).then_inc(PE.h, 1)
                    t.wait_ge(AE.h, C[("Bev", c)])
                    for kc in range(KD):
                        t.matmul(Pm[0].ap()[0:RK, :],
                                 WxTb.ap()[:, kc * 80:kc * 80 + RK],
                                 xnT.ap()[:, kc * T + c * TCC:kc * T + (c + 1) * TCC],
                                 start=(kc == 0), stop=(kc == KD - 1)
                                 ).then_inc(PE.h, 1)
                    t.wait_ge(GPD.h, EX[("wdn", kk)])
                    for lin in range(8):
                        pe_down_lin(lin)
                    for i in range(KD):
                        for n in (2 * i, 2 * i + 1):
                            if n == 0:
                                t.wait_ge(AE.h, C[("dlev", c)])
                            elif n == 1:
                                t.wait_ge(AE.h, C[("Cev", c)])
                            else:
                                t.wait_ge(AE.h, C[("bbe", c, n - 2)])
                            t.matmul(Pm[n % 2].ap(),
                                     e16b.ap()[:, n * P:(n + 1) * P], Btf.ap(),
                                     start=True, stop=True).then_inc(PE.h, 1)
                        pe_down_lin(8 + i)
                    for i in range(KD):
                        for n in (2 * i, 2 * i + 1):
                            if n < 2:
                                t.wait_ge(AE.h, C[("bbe", c, DS - 2 + n)])
                            else:
                                t.wait_ge(AE.h, C[("cbe", c, n - 2)])
                            t.matmul(Pm[n % 2].ap(),
                                     e16b.ap()[:, n * P:(n + 1) * P], Ctf.ap(),
                                     start=True, stop=True).then_inc(PE.h, 1)
                        pe_down_lin(16 + i)
                    for dt in range(2):
                        pe_dpre(t, c, dt)
                    for lin in range(24, 32):
                        pe_down_lin(lin)
                for dt in range(2, KD):
                    pe_dpre(t, NCH - 1, dt)
            S.emit("pe", m_pe)

            def ae_r(a, c, dt):
                a.wait_ge(PE.h, C[("dpre", c, dt)])
                if c >= 1:
                    a.wait_ge(VE.h, C[("u", c - 1, dt)])
                a.activation(delt.ap()[:, dt * TCC:(dt + 1) * TCC],
                             Pm[dt % 2].ap(), AF.Relu,
                             bias=bdtb.ap()[:, dt:dt + 1]).then_inc(AE.h, 1)

            def ae_sp1(a, c, dt):
                a.wait_ge(PE.h, C[("dpre", c, dt)])
                if dt >= 1:
                    a.wait_ge(VE.h, C[("del", c, dt - 1)])
                elif c >= 1:
                    a.wait_ge(VE.h, C[("del", c - 1, KD - 1)])
                a.activation(spscr[0].ap(), Pm[dt % 2].ap(),
                             AF.Abs, bias=bdtb.ap()[:, dt:dt + 1]).then_inc(AE.h, 1)

            def ae_sp2(a, c, dt):
                a.wait_ge(AE.h, C[("sp1", c, dt)])
                a.activation(spscr[1].ap(), spscr[0].ap(), AF.Exp,
                             scale=-1.0).then_inc(AE.h, 1)

            def ae_sp3(a, c, dt):
                a.wait_ge(AE.h, C[("sp2", c, dt)])
                a.activation(spscr[0].ap(), spscr[1].ap(), AF.Ln,
                             bias=1.0).then_inc(AE.h, 1)

            def ae_a1(a, c, dt, n):
                lin = dt * DS + n
                if n == 0:
                    a.wait_ge(VE.h, C[("del", c, dt)])
                if lin >= 6:
                    pdt, pn = divmod(lin - 6, DS)
                    a.wait_ge(VE.h, C[("scan", c, pdt, pn)])
                elif c >= 1:
                    pdt, pn = divmod(KD * DS - 6 + lin, DS)
                    a.wait_ge(VE.h, C[("scan", c - 1, pdt, pn)])
                a.activation(abuf[lin % 6].ap(),
                             delt.ap()[:, dt * TCC:(dt + 1) * TCC],
                             AF.Exp,
                             scale=Ab.ap()[:, dt * DS + n:dt * DS + n + 1]
                             ).then_inc(AE.h, 1)

            def ae_ga(a, c, dt):
                for n in range(6):
                    ae_a1(a, c, dt, n)
                if dt < KD - 1:
                    ae_r(a, c, dt + 1)
                    ae_sp1(a, c, dt + 1)
                ae_a1(a, c, dt, 6)
                if dt < KD - 1:
                    ae_sp2(a, c, dt + 1)
                ae_a1(a, c, dt, 7)
                if dt < KD - 1:
                    ae_sp3(a, c, dt + 1)
                for n in range(8, DS):
                    ae_a1(a, c, dt, n)

            def m_act(a):
                for slot in range(KD):
                    c = slot
                    for ki, k in enumerate((2 * slot, 2 * slot + 1)):
                        for h in range(4):
                            th, q = divmod(h, 2)
                            a.wait_ge(PE.h, EX[("cb", k, h)])
                            a.copy(cbf.ap()[:, th * 1024 + q * 512:
                                            th * 1024 + (q + 1) * 512],
                                   U[h % 4].ap()).then_inc(AE.h, 1)
                        for s in range(KD):
                            a.wait_ge(PE.h, EX[("gate", k, s)])
                            lin = k * KD + s
                            if lin >= 2:
                                pk, ps_ = divmod(lin - 2, KD)
                                a.wait_ge(VE.h, EX[("hid", pk, ps_)])
                            a.activation(sigb2[s % 2].ap(), PB.ap(), AF.Silu
                                         ).then_inc(AE.h, 1)
                            if c >= 1 and s % 2 == 0:
                                ae_ga(a, c - 1, ki * 4 + s // 2)
                    a.wait_ge(PE.h, C[("bcB", c)])
                    a.copy(Btf.ap(), Pm[0].ap()[0:NE, :]).then_inc(AE.h, 1)
                    a.wait_ge(PE.h, C[("bcC", c)])
                    a.copy(Ctf.ap(), Pm[1].ap()[0:NE, :]).then_inc(AE.h, 1)
                    a.wait_ge(PE.h, C[("dlow", c)])
                    a.copy(dlowb.ap(), Pm[0].ap()[0:RK, :]).then_inc(AE.h, 1)
                    for n in range(DS):
                        if n == 0 and c >= 1:
                            a.wait_ge(VE.h, C[("ball", c - 1, KD - 1)])
                        a.wait_ge(PE.h, C[("bbm", c, n)])
                        a.copy(Bball.ap()[:, n * TCC:(n + 1) * TCC],
                               Pm[n % 2].ap()).then_inc(AE.h, 1)
                    for n in range(DS):
                        if n == 0 and c >= 1:
                            a.wait_ge(VE.h, C[("M", c - 1, KD - 1)])
                        a.wait_ge(PE.h, C[("cbm", c, n)])
                        a.copy(Cball.ap()[:, n * TCC:(n + 1) * TCC],
                               Pm[n % 2].ap()).then_inc(AE.h, 1)
                    ae_r(a, c, 0)
                    ae_sp1(a, c, 0)
                    ae_sp2(a, c, 0)
                    ae_sp3(a, c, 0)
                for dt in range(KD):
                    ae_ga(a, NCH - 1, dt)
                a.wait_ge(VE.h, EX[("pa", NE - 1, 31)])
                for dt in range(KD):
                    a.activation(accb.ap()[:, dt * T:(dt + 1) * T],
                                 accb.ap()[:, dt * T:(dt + 1) * T], AF.Silu
                                 ).then_inc(AE.h, 1)
            S.emit("act", m_act)

            def ve_ev_item(v, m):
                kind = m[0]
                if kind == "upch":
                    _, k, s, q = m
                    th = s % 2
                    v.wait_ge(PE.h, EX[("up", k, s, q)])
                    if q == 0:
                        lin = k * KD + s
                        if lin >= 2:
                            pk, ps_ = divmod(lin - 2, KD)
                            v.wait_ge(VE.h, EX[("hid", pk, ps_)])
                    v.tensor_tensor(upcb[s % 2].ap()[:, q * 512:(q + 1) * 512],
                                    U[(2 * s + q) % 4].ap(),
                                    cbf.ap()[:, th * 1024 + q * 512:
                                             th * 1024 + (q + 1) * 512],
                                    ALU.mult).then_inc(VE.h, 1)
                elif kind == "hid":
                    _, k, s = m
                    et, th = s // 2, s % 2
                    v.wait_ge(AE.h, EX[("sig", k, s)])
                    if k % 2 == 0 and k >= 2 and s == 0:
                        v.wait_ge(PE.h, pdlast(k))
                    v.tensor_tensor(
                        hidp[k % 2].ap()[:, et * T + th * 1024:
                                         et * T + (th + 1) * 1024],
                        sigb2[s % 2].ap(), upcb[s % 2].ap(), ALU.mult
                    ).then_inc(VE.h, 1)
                else:
                    _, kk, lin = m
                    dt, r = divmod(lin, 4)
                    th, q = divmod(r, 2)
                    v.wait_ge(PE.h, EX[("pd", kk, lin)])
                    asl = accb.ap()[:, dt * T + th * 1024 + q * 512:
                                    dt * T + th * 1024 + (q + 1) * 512]
                    v.tensor_tensor(asl, U[lin % 4].ap(), asl, ALU.add
                                    ).then_inc(VE.h, 1)

            def ve_m_pre(v, c, dt):
                dsl = delt.ap()[:, dt * TCC:(dt + 1) * TCC]
                v.tensor_tensor(ub1.ap(), dsl,
                                xnT.ap()[:, dt * T + c * TCC:
                                         dt * T + (c + 1) * TCC],
                                ALU.mult).then_inc(VE.h, 1)
                v.wait_ge(AE.h, C[("bbe", c, DS - 1)])
                v.wait_ge(VE.h, C[("u", c, dt)])
                if dt >= 1:
                    v.wait_ge(VE.h, C[("red", c, dt - 1)])
                elif c >= 1:
                    v.wait_ge(VE.h, C[("red", c - 1, KD - 1)])
                v.tensor_tensor(
                    ball.ap().rearrange("p (n t) -> p n t", n=DS),
                    Bball.ap().rearrange("p (n t) -> p n t", n=DS),
                    ub1.ap().unsqueeze(1).broadcast_to([P, DS, TCC]),
                    ALU.mult).then_inc(VE.h, 1)

            def ve_m_scan(v, c, dt, n):
                v.wait_ge(VE.h, C[("ball", c, dt)])
                v.wait_ge(AE.h, C[("a", c, dt, n)])
                v.tensor_tensor_scan(
                    ball.ap()[:, n * TCC:(n + 1) * TCC],
                    abuf[(dt * DS + n) % 6].ap(),
                    ball.ap()[:, n * TCC:(n + 1) * TCC],
                    hcarry.ap()[:, dt * DS + n:dt * DS + n + 1],
                    ALU.mult, ALU.add).then_inc(VE.h, 1)

            def ve_m_post(v, c, dt):
                v.wait_ge(VE.h, C[("scan", c, dt, DS - 1)])
                v.tensor_copy(
                    hcarry.ap()[:, dt * DS:(dt + 1) * DS].unsqueeze(2),
                    ball.ap().rearrange("p (n t) -> p n t", n=DS)[
                        :, :, TCC - 1:TCC]).then_inc(VE.h, 1)
                v.wait_ge(AE.h, C[("cbe", c, DS - 1)])
                v.wait_ge(VE.h, C[("carry", c, dt)])
                v.tensor_tensor(ball.ap(), ball.ap(), Cball.ap(), ALU.mult
                                ).then_inc(VE.h, 1)
                hv = lambda a_, b_: ball.ap().rearrange(
                    "p (n t) -> p n t", n=DS)[:, a_:b_, :]
                v.wait_ge(VE.h, C[("M", c, dt)])
                v.tensor_tensor(hv(0, 8), hv(0, 8), hv(8, 16), ALU.add
                                ).then_inc(VE.h, 1)
                v.wait_ge(VE.h, C[("t1", c, dt)])
                v.tensor_tensor(hv(0, 4), hv(0, 4), hv(4, 8), ALU.add
                                ).then_inc(VE.h, 1)
                v.wait_ge(VE.h, C[("t2", c, dt)])
                v.tensor_tensor(hv(0, 2), hv(0, 2), hv(2, 4), ALU.add
                                ).then_inc(VE.h, 1)
                v.wait_ge(VE.h, C[("t3", c, dt)])
                if dt == 0 and c >= 1:
                    v.wait_ge(YS.h, C[("ys", c - 1)])
                v.tensor_tensor(yb8.ap()[:, dt * TCC:(dt + 1) * TCC],
                                ball.ap()[:, 0:TCC],
                                ball.ap()[:, TCC:2 * TCC], ALU.add
                                ).then_inc(VE.h, 1)

            def ve_del(v, c, dt):
                dsl = delt.ap()[:, dt * TCC:(dt + 1) * TCC]
                v.wait_ge(AE.h, C[("sp3", c, dt)])
                v.tensor_tensor(dsl, dsl, spscr[0].ap(), ALU.add
                                ).then_inc(VE.h, 1)

            def ve_m_group(v, c, dt, e6):
                if e6 is not None:
                    ve_ev_item(v, e6[0])
                    ve_ev_item(v, e6[1])
                ve_del(v, c, dt)
                ve_m_pre(v, c, dt)
                if e6 is not None:
                    ve_ev_item(v, e6[2])
                for n in range(0, 3):
                    ve_m_scan(v, c, dt, n)
                if e6 is not None:
                    ve_ev_item(v, e6[3])
                    ve_ev_item(v, e6[4])
                for n in range(3, DS):
                    ve_m_scan(v, c, dt, n)
                if e6 is not None:
                    ve_ev_item(v, e6[5])
                ve_m_post(v, c, dt)

            def m_dve(v):
                for slot in range(KD):
                    c = slot
                    ev = ve_ev_marks(slot)
                    if c == 0:
                        for m in ev:
                            ve_ev_item(v, m)
                    else:
                        for dt in range(KD):
                            ve_m_group(v, c - 1, dt, ev[6 * dt:6 * dt + 6])
                    for m in ve_pa_marks(slot):
                        ve_ev_item(v, m)
                for dt in range(KD):
                    ve_m_group(v, NCH - 1, dt, None)
            S.emit("dve", m_dve)

            def m_sp(s):
                for c in range(NCH):
                    s.wait_ge(VE.h, C[("red", c, KD - 1)])
                    s.dma_start(
                        yscr_d.rearrange("p (j t2) -> p j t2", t2=T)[
                            :, :, c * TCC:(c + 1) * TCC],
                        yb8.ap().rearrange("p (j t2) -> p j t2", t2=TCC)
                    ).then_inc(YS.h, 16)
            S.emit("sp", m_sp)

            if DEBUG:
                DBG.bump(16)

                def dbg_b(g):
                    g.wait_ge(VE.h, EX[("pa", NE - 1, 31)])
                    g.dma_start(dacc_d, accb.ap()).then_inc(DBG.h, 16)
                    g.wait_ge(DBG.h, 48)
                S.emit("pool", dbg_b)

            S.run_block(nc)

        with ExitStack() as pt:
            sbt = lambda n_, s_, d_: pt.enter_context(nc.sbuf_tensor(n_, s_, d_))
            NTW = TCW // P
            xo = [sbt(f"xo{i}", [P, NTW * D], F32) for i in range(2)]
            ylb = [sbt(f"ylb{i}", [P, KD * TCW], BF16) for i in range(2)]
            sgf = [sbt(f"sgf{i}", [P, TCW], F32) for i in range(2)]
            Ptf2 = [pt.enter_context(nc.psum_tensor(f"Ptf2{i}", [P, P], F32))
                    for i in range(2)]

            CT = {}
            for cc in range(NCW):
                CT[("x", cc)] = SPD.bump(NTW * 16)
            for cc in range(NCW):
                CT[("y", cc)] = GPD.bump(16)
            for cc in range(NCW):
                for dt in range(KD):
                    CT[("sg", cc, dt)] = VE.bump()
                    for tau in range(NTW):
                        CT[("add", cc, dt, tau)] = VE.bump()
            for cc in range(NCW):
                for dt in range(KD):
                    for tau in range(NTW):
                        CT[("sgt", cc, dt, tau)] = PE.bump()
            for cc in range(NCW):
                CT[("out", cc)] = OUT.bump(NTW * 16)

            def t_sp(s):
                for cc in range(NCW):
                    if cc >= 2:
                        s.wait_ge(OUT.h, CT[("out", cc - 2)])
                    for tau in range(NTW):
                        tg = cc * NTW + tau
                        s.dma_start(xo[cc % 2].ap()[:, tau * D:(tau + 1) * D],
                                    x_d[tg * P:(tg + 1) * P, :]).then_inc(SPD.h, 16)
                s.wait_ge(OUT.h, CT[("out", NCW - 1)])
            S.emit("sp", t_sp)

            def t_pool(g):
                for cc in range(NCW):
                    if cc >= 2:
                        g.wait_ge(VE.h, CT[("sg", cc - 2, KD - 1)])
                    g.wait_ge(YS.h, C[("ys", (cc + 1) * (TCW // TCC) - 1)])
                    g.dma_start(
                        ylb[cc % 2].ap().rearrange("p (j t2) -> p j t2", t2=TCW),
                        yscr_d.rearrange("p (j t2) -> p j t2", t2=T)[
                            :, :, cc * TCW:(cc + 1) * TCW]).then_inc(GPD.h, 16)
            S.emit("pool", t_pool)

            def t_act(a):
                for cc in range(NCW):
                    for tau in range(NTW):
                        tg = cc * NTW + tau
                        a.wait_ge(VE.h, CT[("add", cc, KD - 1, tau)])
                        a.dma_start(out_d[tg * P:(tg + 1) * P, :],
                                    xo[cc % 2].ap()[:, tau * D:(tau + 1) * D]
                                    ).then_inc(OUT.h, 16)
            S.emit("act", t_act)

            def t_dve(v):
                for cc in range(NCW):
                    v.wait_ge(GPD.h, CT[("y", cc)])
                    for dt in range(KD):
                        if dt >= 2:
                            v.wait_ge(PE.h, CT[("sgt", cc, dt - 2, NTW - 1)])
                        elif cc >= 1:
                            v.wait_ge(PE.h, CT[("sgt", cc - 1, KD - 2 + dt, NTW - 1)])
                        v.tensor_tensor(sgf[dt % 2].ap(),
                                        ylb[cc % 2].ap()[:, dt * TCW:(dt + 1) * TCW],
                                        accb.ap()[:, dt * T + cc * TCW:
                                                  dt * T + (cc + 1) * TCW],
                                        ALU.mult).then_inc(VE.h, 1)
                        if dt == 0:
                            v.wait_ge(SPD.h, CT[("x", cc)])
                        for tau in range(NTW):
                            v.wait_ge(PE.h, CT[("sgt", cc, dt, tau)])
                            xslc = xo[cc % 2].ap()[
                                :, tau * D + dt * P:tau * D + (dt + 1) * P]
                            v.tensor_tensor(xslc,
                                            Ptf2[(dt * NTW + tau) % 2].ap(),
                                            xslc, ALU.add).then_inc(VE.h, 1)
            S.emit("dve", t_dve)

            def t_pe(t):
                t.wait_ge(AE.h, gfin)
                for cc in range(NCW):
                    for dt in range(KD):
                        t.wait_ge(VE.h, CT[("sg", cc, dt)])
                        for tau in range(NTW):
                            lin = dt * NTW + tau
                            if lin >= 2:
                                pdt, ptau = divmod(lin - 2, NTW)
                                t.wait_ge(VE.h, CT[("add", cc, pdt, ptau)])
                            elif cc >= 1:
                                t.wait_ge(VE.h,
                                          CT[("add", cc - 1, KD - 1,
                                              NTW - 2 + lin)])
                            t.transpose(Ptf2[lin % 2].ap(),
                                        sgf[dt % 2].ap()[:, tau * P:(tau + 1) * P],
                                        idf.ap()).then_inc(PE.h, 1)
            S.emit("pe", t_pe)

            S.run_block(nc)

    return nc


def _consts():
    iota = np.broadcast_to(np.arange(NE, dtype=np.float32), (P, NE)).copy()
    e16 = np.zeros((NE, NE * P), np.float32)
    for n in range(NE):
        e16[n, n * P:(n + 1) * P] = 1.0
    return {
        "identf": np.eye(P, dtype=np.float32),
        "identb": np.eye(P, dtype=ml_dtypes.bfloat16),
        "iota16": iota,
        "iota100": iota + 100.0,
        "e16": e16,
    }


def kernel(L0_attn_x, L0_attn_A_log, L0_attn_W_x, L0_attn_W_dt, L0_attn_b_dt,
           L0_attn_W_C, L0_ffn_router_w, L0_ffn_router_bias, L0_ffn_up_proj,
           L0_ffn_gate_proj, L0_ffn_down_proj, L0_ffn_sh_gate, L0_ffn_sh_up,
           L0_ffn_sh_down):
    x = np.ascontiguousarray(np.asarray(L0_attn_x, dtype=np.float32))
    shared = {
        "A_log": np.ascontiguousarray(L0_attn_A_log, dtype=np.float32),
        "W_x": np.ascontiguousarray(L0_attn_W_x, dtype=np.float32),
        "W_dt": np.ascontiguousarray(L0_attn_W_dt, dtype=np.float32),
        "b_dt": np.ascontiguousarray(L0_attn_b_dt, dtype=np.float32),
        "W_C": np.ascontiguousarray(L0_attn_W_C, dtype=np.float32),
        "router_w": np.ascontiguousarray(L0_ffn_router_w, dtype=np.float32),
        "router_bias": np.ascontiguousarray(L0_ffn_router_bias, dtype=np.float32),
        "up_proj": np.ascontiguousarray(L0_ffn_up_proj, dtype=np.float32),
        "gate_proj": np.ascontiguousarray(L0_ffn_gate_proj, dtype=np.float32),
        "down_proj": np.ascontiguousarray(L0_ffn_down_proj, dtype=np.float32),
        "sh_gate": np.ascontiguousarray(L0_ffn_sh_gate, dtype=np.float32),
        "sh_up": np.ascontiguousarray(L0_ffn_sh_up, dtype=np.float32),
        "sh_down": np.ascontiguousarray(L0_ffn_sh_down, dtype=np.float32),
        **_consts(),
    }
    nc = build_nc()
    in_maps = [dict(shared, x=np.ascontiguousarray(x[c])) for c in range(8)]
    res = run_bass_kernel_spmd(nc, in_maps, core_ids=list(range(8)))
    out = np.stack([res.results[c]["out"] for c in range(8)], axis=0)
    return out.astype(np.float32)
